# version 12
# speedup vs baseline: 1.0665x; 1.0091x over previous
import numpy as np
import ml_dtypes
from contextlib import ExitStack
import concourse.bass as bass
import concourse.mybir as mybir
from concourse.bass_utils import run_bass_kernel_spmd

F32 = mybir.dt.float32
BF16 = mybir.dt.bfloat16
I32 = mybir.dt.int32
AF = mybir.ActivationFunctionType
ALU = mybir.AluOpType

D = 2048
NT = 2048
NTILE = 16
DTL = 16
FF = 8192
GROUPS = 4
NSC = 4
CH = 256
PI = float(np.pi)
TWO_PI = float(2 * np.pi)
LN1E4 = float(np.log(10000.0))

_STOP = None


class Sem:
    def __init__(self, es, nc, name, engine_sem=False):
        self.h = es.enter_context(nc.semaphore(name))
        self.n = 0
        self.engine_sem = engine_sem

    def plan(self, k=1):
        self.n += k
        return self.n


class Tok:
    __slots__ = ("eng", "sem", "val")

    def __init__(self, eng, sem, val):
        self.eng, self.sem, self.val = eng, sem, val


class Sched:
    ENGS = ("sync", "tensor", "vector", "scalar", "gpsimd")

    def __init__(self, es, nc):
        self.nc = nc
        self.ops = {e: [] for e in self.ENGS}
        self.last_is_drain = {e: False for e in self.ENGS}
        self.wm = {e: {} for e in self.ENGS}
        self.esems = {e: Sem(es, nc, "se_" + e, engine_sem=True) for e in
                      ("tensor", "vector", "scalar", "gpsimd")}

    def _raw(self, eng, fn):
        self.ops[eng].append(fn)
        self.last_is_drain[eng] = False

    def drain(self, eng):
        if not self.last_is_drain[eng]:
            self.ops[eng].append(lambda e: e.drain())
            self.last_is_drain[eng] = True

    def wait(self, eng, sem, val):
        key = id(sem.h)
        if self.wm[eng].get(key, 0) >= val:
            return
        self.wm[eng][key] = val
        self._raw(eng, lambda e, s=sem.h, v=val: e.wait_ge(s, v))

    def dep(self, eng, deps):
        for t in deps:
            if t is None:
                continue
            if t.sem is None:
                assert t.eng == eng, "cross-engine dep without semaphore"
                self.drain(eng)
            elif t.eng == eng and t.sem.engine_sem:
                self.drain(eng)
            else:
                self.wait(eng, t.sem, t.val)

    def op(self, eng, fn, deps=(), inc=False):
        self.dep(eng, deps)
        if inc:
            sem = self.esems[eng]
            v = sem.plan(1)
            self._raw(eng, lambda e, f=fn, s=sem.h: f(e).then_inc(s, 1))
            return Tok(eng, sem, v)
        self._raw(eng, fn)
        return Tok(eng, None, None)

    def dma(self, q, out, in_, sem, deps=()):
        self.dep(q, deps)
        v = sem.plan(16)
        self._raw(q, lambda e, o=out, i=in_, s=sem.h:
                  e.dma_start(out=o, in_=i).then_inc(s, 16))
        return Tok(q, sem, v)

    def cc(self, kind, alu, groups, ins, outs, sem, deps=()):
        self.dep("gpsimd", deps)
        v = sem.plan(1)
        self._raw("gpsimd", lambda e: e.collective_compute(
            kind, alu, replica_groups=groups, ins=ins, outs=outs).then_inc(sem.h, 1))
        return Tok("gpsimd", sem, v)


def build(stop=None):
    nc = bass.Bass("TRN2", target_bir_lowering=False)
    es = ExitStack()

    def xin(name, shape, dt=F32):
        return nc.dram_tensor(name, shape, dt, kind="ExternalInput").ap()

    x_x = xin("x", [NT, D])
    x_w1sl = xin("w1t_sl", [256, FF])
    x_w2sl = xin("w2t_sl", [1024, D])
    x_wpT = xin("wpT", [D, 3 * D])
    x_woT = xin("woT", [D, D])
    x_nsr = xin("nsr", [1, D])
    x_cwc = xin("cwc", [128, 64])
    x_cbc = xin("cbc", [128, 16])
    x_dtbc = xin("dtbc", [128, 16])
    x_b1c = xin("b1c", [128, 64])
    x_b2r = xin("b2r", [1, D])
    x_hmr = xin("hmr", [1, 256])
    x_posc = xin("posc", [128, 16])
    x_eye = xin("eye", [128, 128], BF16)
    x_ones1 = xin("ones1", [1, 128], BF16)
    x_tri = xin("tri32", [32, 32], BF16)
    x_stri = xin("stri32", [32, 32], BF16)
    x_hsel = xin("hsel", [24, 3])
    x_csel = xin("csel", [32, 8])
    x_out = nc.dram_tensor("out", [NT, D], F32, kind="ExternalOutput").ap()

    def dint(name, shape, dt):
        return nc.dram_tensor(name, shape, dt, kind="Internal").ap()

    def dsh(name, shape, dt):
        return nc.dram_tensor(name, shape, dt, kind="Internal",
                              addr_space="Shared").ap()

    d_w1qinA = dint("w1q_inA", [256, FF // 2], BF16)
    d_w1qinB = dint("w1q_inB", [256, FF // 2], BF16)
    d_w1qA = dsh("w1qA", [D, FF // 2], BF16)
    d_w1qB = dsh("w1qB", [D, FF // 2], BF16)
    d_w2qin_e = [dint("w2q_in%d" % e, [1024, 512], BF16) for e in range(4)]
    d_w2q_e = [dsh("w2q%d" % e, [FF, 512], BF16) for e in range(4)]
    d_arin = dint("ar_in", [1, 2], F32)
    d_arout = dint("ar_out", [1, 2], F32)
    d_arin2 = dint("ar_in2", [1, 2], F32)
    d_arout2 = dint("ar_out2", [1, 2], F32)
    d_red = dint("red_d", [128, 2], F32)
    d_red2 = dint("red2_d", [128, 8], F32)
    d_wpbf = dint("wp_bf", [D, 3 * D], BF16)
    d_wobf = dint("wo_bf", [D, D], BF16)
    d_hmi = dint("hmi_d", [16 * 128, D], BF16)
    d_x1 = dint("x1_d", [NT, D], F32)
    d_x2 = dint("x2_d", [NT, D], F32)
    d_co = dint("co_d", [8 * 128, DTL * CH], BF16)
    d_es = dint("es_d", [8 * 128, DTL * CH], BF16)
    d_ga = dint("ga_d", [8 * 128, DTL * CH], BF16)
    d_hin = dint("halo_in", [3, D], BF16)
    d_hout = dint("halo_out", [24, D], BF16)
    d_cin = dint("agc_in", [16, D], F32)
    d_cout = dint("agc_out", [64, D], F32)

    def sbt(name, shape, dt=F32):
        return es.enter_context(nc.sbuf_tensor(name, shape, dt))

    s_eye = sbt("s_eye", [128, 128], BF16)
    s_ones1 = sbt("s_ones1", [1, 128], BF16)
    s_tri = sbt("s_tri", [32, 32], BF16)
    s_stri = sbt("s_stri", [32, 32], BF16)
    s_hsel = sbt("s_hsel", [24, 3], BF16)
    s_csel = sbt("s_csel", [32, 8], BF16)
    s_cwc = sbt("s_cwc", [128, 64])
    s_cbc = sbt("s_cbc", [128, 16])
    s_dtbc = sbt("s_dtbc", [128, 16])
    s_b1c = sbt("s_b1c", [128, 64])
    s_b2h = sbt("s_b2h", [1, D], BF16)
    s_nsb = sbt("s_nsb", [128, D])
    s_hmb = sbt("s_hmb", [128, 256])
    s_diag = sbt("s_diag", [128, 64 * 128], BF16)
    s_cosT = sbt("s_cosT", [128, NTILE * 64])
    s_sinT = sbt("s_sinT", [128, NTILE * 64])
    s_posc = sbt("s_posc", [128, 16])
    s_invf = sbt("s_invf", [128, 64])
    s_iw1 = sbt("s_iw1", [128, 64])
    s_iw2 = sbt("s_iw2", [128, 64])
    s_iw3 = sbt("s_iw3", [128, 64])
    s_iint = sbt("s_iint", [128, 64], I32)
    s_stat = sbt("s_stat", [128, 64])
    s_rstdA = sbt("s_rstdA", [128, 16])
    s_meanA = sbt("s_meanA", [128, 16])
    s_bdA = sbt("s_bdA", [128, 128])
    s_boA = sbt("s_boA", [128, 128])
    s_lmA = sbt("s_lmA", [128, 128])
    s_corr = sbt("s_corr", [128, 128])
    s_cscr = sbt("s_cscr", [128, 128])
    s_bdh = sbt("s_bdh", [128, 256], BF16)
    s_boh = sbt("s_boh", [128, 128], BF16)
    s_scl = sbt("s_scl", [128, 16])
    s_thr = sbt("s_thr", [128, 2])
    s_halo = sbt("s_halo", [128, 48], BF16)
    s_negl = sbt("s_negl", [128, 32])
    s_hmst = sbt("s_hmst", [128, 2048], BF16)
    s_one = sbt("s_one", [128, 256])

    AW = 39600
    AR = sbt("ARENA", [128, AW])
    ARB = AR[:].bitcast(BF16)

    def fv(off, n):
        assert 0 <= off and off + n <= AW, (off, n)
        return AR[:, off:off + n]

    def bvv(offw, n):
        assert 0 <= offw and 2 * offw + n <= 2 * AW, (offw, n)
        return ARB[:, 2 * offw:2 * offw + n]

    P2A, P2B, P2ABS, P2GE, P2OUT = 0, 8192, 16384, 24576, 32768
    NSRO, HMRO, RED1O = 36864, 38912, 36864
    NSHLO, HMRBO = 32768, 34816
    AXG, AH, AXN1, AXN1T = 0, 8192, 24576, 25600
    AW1S, AW2S = 29696, 33792
    ASCR = 8192
    BDT0, BDT1, BLB = 0, 4096, 8192
    BXN2T = 12288
    BSA = 12288
    BXCT = 16512
    BSB = 16512
    BES = 20608
    BVROT = 24704
    BVV4 = 28800
    BCO = 28800
    BVR = 32896
    BCREP = 33920
    BWPS = 34944
    BGA = 39040
    BAGH = 8192
    KC = dict(a=0, b=2048, c=4096, e=6144, h=8192, c8=10240, c8h=12288, s=13312)
    CFIN, CVMG, CCO, CES, CGA = 0, 4096, 8192, 12288, 16384
    CX1, CT, CHM, CWOS = 20480, 28672, 30720, 32768
    assert CWOS + 4096 <= AW

    PSBIG = es.enter_context(nc.psum_tensor("PSBIG", [128, 2048], F32))[:]
    PS2a = es.enter_context(nc.psum_tensor("PS2a", [128, 512], F32))[:]
    PS2b = es.enter_context(nc.psum_tensor("PS2b", [128, 512], F32))[:]
    PSTa = es.enter_context(nc.psum_tensor("PSTa", [128, 1024], BF16))[:]
    PSTb = es.enter_context(nc.psum_tensor("PSTb", [128, 1024], BF16))[:]
    PST = [PSTa, PSTb]

    def psb(k):
        return PSBIG[:, 512 * k:512 * (k + 1)]

    s_hcol_negl = s_negl[:, 16:32]
    sch = Sched(es, nc)

    def S(name):
        return Sem(es, nc, name)

    s_ar, s_agw1, s_agw2 = S("s_ar"), S("s_agw1"), S("s_agw2")
    s_ar2, s_agw1b = S("s_ar2"), S("s_agw1b")
    s_agh_s, s_agc_s = S("s_aghs"), S("s_agcs")
    d_small, d_gc, d_cast, d_hmw = S("d_small"), S("d_gc"), S("d_cast"), S("d_hmw")
    d_q, d_q2 = S("d_q"), S("d_q2")
    d_qb, d_q2b = S("d_qb"), S("d_q2b")
    d_spl, d_cbg = S("d_spl"), S("d_cbg")
    d_xa, d_x1w = S("d_xa"), S("d_x1w")
    d_w1 = [S("d_w1a"), S("d_w1b")]
    d_w2 = [S("d_w2a"), S("d_w2b")]
    d_wp = [S("d_wpa"), S("d_wpb")]
    d_wo = [S("d_woa"), S("d_wob")]
    d_hmd = [S("d_hmda"), S("d_hmdb")]
    d_xb, d_xh = S("d_xb"), S("d_xh")
    d_rlA, d_rlB, d_rlC = S("d_rlA"), S("d_rlB"), S("d_rlC")
    d_gasp = [S("d_gaspa"), S("d_gaspb")]
    d_cosp = S("d_cosp")
    d_x2w, d_xc, d_out, d_cbn = S("d_x2w"), S("d_xc"), S("d_out"), S("d_cbn")
    d_r1, d_r2, d_ai, d_ai2 = S("d_r1"), S("d_r2"), S("d_ai"), S("d_ai2")
    d_td, d_td2 = S("d_td"), S("d_td2")

    OP, DMA, CC = sch.op, sch.dma, sch.cc
    V, SC_, T, G, SY = "vector", "scalar", "tensor", "gpsimd", "sync"
    VD = Tok(V, None, None)
    SD = Tok(SC_, None, None)

    nsr_r = AR[0:1, NSRO:NSRO + D]
    hmr_r = AR[0:1, HMRO:HMRO + 256]
    hmrb_r = ARB[0:1, 2 * HMRBO:2 * HMRBO + 256]
    nshl_r = ARB[0:1, 2 * NSHLO:2 * NSHLO + 2 * D]
    red1_r = AR[64:65, RED1O:RED1O + D]
    agh_r = ARB[0:24, 2 * 36992:2 * 36992 + D]


    t_small = None
    for dst, src in [
        (s_cwc[:], x_cwc), (s_cbc[:], x_cbc), (s_dtbc[:], x_dtbc),
        (s_b1c[:], x_b1c), (nsr_r, x_nsr), (hmr_r, x_hmr),
        (s_posc[:], x_posc), (s_eye[:], x_eye), (s_ones1[:], x_ones1),
        (s_tri[:], x_tri), (s_stri[:], x_stri), (red1_r, x_b2r),
    ]:
        t_small = DMA(SY, dst, src, d_small)
    t_b2l = t_small
    DMA(G, s_hsel[:], x_hsel, d_cast)
    DMA(G, s_csel[:], x_csel, d_cast)
    t_iota = OP(G, lambda e: e.iota(s_iint[:], pattern=[[1, 64]], base=0,
                                    channel_multiplier=0), inc=True)
    t_one = OP(V, lambda e: e.memset(s_one[:], 1.0), inc=True)
    t_thr0 = OP(V, lambda e: e.memset(s_thr[:, 0:2], 0.0), deps=[VD], inc=True)

    t_b2h = OP(V, lambda e: e.tensor_scalar(out=s_b2h[:], in0=red1_r,
        scalar1=0.5, scalar2=None, op0=ALU.mult), deps=[t_b2l], inc=True)
    t_nh = OP(V, lambda e: e.tensor_copy(nshl_r[0:1, 0:D], nsr_r), deps=[t_small])
    t_nl = OP(V, lambda e: e.tensor_tensor(out=nshl_r[0:1, D:2 * D], in0=nsr_r,
        in1=nshl_r[0:1, 0:D], op=ALU.subtract), deps=[VD], inc=True)
    t_nsmm = None
    for q in range(4):
        OP(T, lambda e, qq=q: e.matmul(psb(qq), s_ones1[:],
            nshl_r[0:1, 512 * qq:512 * (qq + 1)], start=True, stop=False),
           deps=[t_nl, t_small])
        t_nsmm = OP(T, lambda e, qq=q: e.matmul(psb(qq), s_ones1[:],
            nshl_r[0:1, D + 512 * qq:D + 512 * (qq + 1)], start=False,
            stop=True), inc=(q == 3))
    t_nsb = OP(V, lambda e: e.tensor_copy(s_nsb[:], PSBIG[:]), deps=[t_nsmm],
               inc=True)
    t_hmc = OP(V, lambda e: e.tensor_copy(hmrb_r, hmr_r), deps=[t_small], inc=True)
    t_hmm = OP(T, lambda e: e.matmul(PS2a[:, 0:256], s_ones1[:], hmrb_r,
        start=True, stop=True), deps=[t_hmc], inc=True)
    t_hmb = OP(V, lambda e: e.tensor_copy(s_hmb[:], PS2a[:, 0:256]),
               deps=[t_hmm], inc=True)
    qA, qB = fv(P2A, 8192), fv(P2B, 8192)
    qabs, qge = fv(P2ABS, 8192), fv(P2GE, 8192)
    qout = bvv(P2OUT, 8192)
    _diagf = s_diag[:].bitcast(F32)
    w2A, w2B = _diagf[:, 0:2048], _diagf[:, 2048:4096]
    w2ge = s_hmst[:, 0:2048]
    w2g2 = s_cosT[:].bitcast(BF16)[:, 0:2048]
    w2a = s_sinT[:].bitcast(BF16)[:, 0:2048]
    w2out = bvv(37888, 2048)

    t_w1l = [None, None]
    for h in range(2):
        t_w1l[h] = DMA(SY, (qA if h == 0 else qB), x_w1sl[128 * h:128 * (h + 1), :],
                       d_q if h == 0 else d_qb)
    t_w2abs = [None] * 8
    t_sgnw2 = [None] * 8
    t_splw2 = [None] * 8
    t_tqw2 = [None] * 8
    for h in range(8):
        buf = w2A if h % 2 == 0 else w2B
        tl = DMA(G, buf, x_w2sl[128 * h:128 * (h + 1), :],
                 d_q2 if h % 2 == 0 else d_q2b,
                 deps=[t_w2abs[h - 2]] if h >= 2 else [])
        t_w2abs[h] = OP(SC_, lambda e, b=buf, hh=h: e.activation(
            out=w2ge, in_=b, func=AF.Abs, accum_out=s_scl[:, 2 + hh:3 + hh]),
            deps=[tl, SD], inc=True)
    t_w1abs = [None, None]
    for h in range(2):
        t_w1abs[h] = OP(SC_, lambda e, b=(qA if h == 0 else qB), hh=h: e.activation(
            out=qabs[:, 0:8192], in_=b, func=AF.Abs,
            accum_out=s_scl[:, hh:hh + 1]),
            deps=[t_w1l[h], SD], inc=True)
    t_rd0 = DMA(G, d_red, s_scl[:, 0:2], d_r1, deps=[t_w1abs[1]])
    t_rd1 = DMA(G, red1_r[0:1, 0:256], d_red.rearrange(
        "p c -> (p c)").unsqueeze(0), d_r1, deps=[t_rd0, t_b2h])
    red3 = red1_r[0:1, 0:256].rearrange("o (p c) -> o p c", c=2)
    t_s1 = OP(V, lambda e: e.reduce_sum(s_thr[0:1, 0:1], red3[:, :, 0:2],
        axis=mybir.AxisListType.XY), deps=[t_rd1, t_thr0])
    t_s3 = OP(V, lambda e: e.tensor_scalar(out=s_thr[0:1, 0:1],
        in0=s_thr[0:1, 0:1], scalar1=0.5 / 16777216.0, scalar2=None,
        op0=ALU.mult), deps=[VD], inc=True)
    t_ari = DMA(G, d_arin, s_thr[0:1, 0:2], d_ai, deps=[t_s3])
    t_ar = CC("AllReduce", ALU.add, [list(range(8))], [d_arin.opt()],
              [d_arout.opt()], s_ar, deps=[t_ari])
    t_rd0b = DMA(G, d_red2, s_scl[:, 2:10], d_r2, deps=[t_w2abs[-1]])
    t_rd1b = DMA(G, red1_r[0:1, 256:1280], d_red2.rearrange(
        "p c -> (p c)").unsqueeze(0), d_r2, deps=[t_rd0b])
    red3b = red1_r[0:1, 256:1280].rearrange("o (p c) -> o p c", c=8)
    t_s2 = OP(V, lambda e: e.reduce_sum(s_thr[0:1, 1:2], red3b,
        axis=mybir.AxisListType.XY), deps=[t_rd1b, t_s3])
    t_s4 = OP(V, lambda e: e.tensor_scalar(out=s_thr[0:1, 1:2],
        in0=s_thr[0:1, 1:2], scalar1=0.5 / 16777216.0, scalar2=None,
        op0=ALU.mult), deps=[VD], inc=True)
    t_ari2 = DMA(G, d_arin2, s_thr[0:1, 0:2], d_ai2, deps=[t_s4])
    t_ar2 = CC("AllReduce", ALU.add, [list(range(8))], [d_arin2.opt()],
               [d_arout2.opt()], s_ar2, deps=[t_ari2])
    t_td = None
    for qd in range(4):
        t_td = DMA(G, s_thr[32 * qd:32 * qd + 1, 0:1], d_arout[0:1, 0:1], d_td,
                   deps=[t_ar, t_thr0] if qd == 0 else [])
    t_td2 = None
    for qd in range(4):
        t_td2 = DMA(G, s_thr[32 * qd:32 * qd + 1, 1:2], d_arout2[0:1, 1:2],
                    d_td2, deps=[t_ar2] if qd == 0 else [])
    t_thr = OP(V, lambda e: e.stream_shuffle(s_thr[:, 0:1], s_thr[:, 0:1],
        mask=[0] * 32), deps=[t_td], inc=True)
    t_thr2 = OP(V, lambda e: e.stream_shuffle(s_thr[:, 1:2], s_thr[:, 1:2],
        mask=[0] * 32), deps=[t_td2, VD], inc=True)
    t_spl1 = []
    t_tq1 = [None, None]
    for h in range(2):
        buf = qA if h == 0 else qB
        OP(V, lambda e, b=buf: e.tensor_scalar(
            out=qge[:, 0:8192], in0=b, scalar1=s_thr[:, 0:1], scalar2=None,
            op0=ALU.is_ge), deps=[t_thr, VD], inc=True)
        OP(V, lambda e, b=buf: e.tensor_scalar(
            out=qabs[:, 0:8192], in0=b, scalar1=-1.0, scalar2=s_thr[:, 0:1],
            op0=ALU.mult, op1=ALU.is_ge), deps=[VD])
        OP(V, lambda e: e.tensor_tensor(out=qge[:, 0:8192],
            in0=qge[:, 0:8192], in1=qabs[:, 0:8192], op=ALU.add), deps=[VD])
        OP(V, lambda e, b=buf: e.tensor_scalar(
            out=qabs[:, 0:8192], in0=b, scalar1=1e38, scalar2=1.0,
            op0=ALU.mult, op1=ALU.min), deps=[VD])
        t_tq1[h] = OP(V, lambda e: e.scalar_tensor_tensor(out=qout[:, 0:8192],
            in0=qabs[:, 0:8192], scalar=-1.0, in1=qge[:, 0:8192],
            op0=ALU.max, op1=ALU.mult),
            deps=[VD, t_spl1[1] if h else None], inc=True)
        t_spl1.append(DMA(G, d_w1qinA[128 * h:128 * (h + 1), :],
                          qout[:, 0:4096], d_spl, deps=[t_tq1[h]]))
        t_spl1.append(DMA(G, d_w1qinB[128 * h:128 * (h + 1), :],
                          qout[:, 4096:8192], d_spl, deps=[]))
    t_agw1 = CC("AllGather", ALU.bypass, [list(range(8))],
                [d_w1qinA.opt()], [d_w1qA.opt()], s_agw1,
                deps=[t_spl1[3]])
    t_agw1b = CC("AllGather", ALU.bypass, [list(range(8))],
                 [d_w1qinB.opt()], [d_w1qB.opt()], s_agw1b,
                 deps=[t_spl1[3]])
    for h in range(8):
        buf = w2A if h % 2 == 0 else w2B
        tl2 = DMA(G, buf, x_w2sl[128 * h:128 * (h + 1), :],
                  d_q2 if h % 2 == 0 else d_q2b,
                  deps=[t_sgnw2[h - 2]] if h >= 2 else [t_w2abs[6 + h]])
        OP(V, lambda e, b=buf: e.tensor_scalar(
            out=w2ge, in0=b, scalar1=s_thr[:, 1:2], scalar2=None,
            op0=ALU.is_ge), deps=[tl2, t_thr2, VD, t_w2abs[7] if h == 0 else None],
            inc=True)
        OP(V, lambda e, b=buf: e.tensor_scalar(
            out=w2g2, in0=b, scalar1=-1.0, scalar2=s_thr[:, 1:2],
            op0=ALU.mult, op1=ALU.is_ge), deps=[VD])
        OP(V, lambda e: e.tensor_tensor(out=w2ge,
            in0=w2ge, in1=w2g2, op=ALU.add), deps=[VD])
        t_sgnw2[h] = OP(V, lambda e, b=buf: e.tensor_scalar(
            out=w2a, in0=b, scalar1=1e38, scalar2=0.5,
            op0=ALU.mult, op1=ALU.min), deps=[VD], inc=True)
        t_tqw2[h] = OP(V, lambda e: e.scalar_tensor_tensor(out=w2out,
            in0=w2a, scalar=-0.5, in1=w2ge, op0=ALU.max, op1=ALU.mult),
            deps=[VD, t_splw2[h - 1] if h else None], inc=True)
        for e4 in range(4):
            t_splw2[h] = DMA(G, d_w2qin_e[e4][128 * h:128 * (h + 1), :],
                             w2out[:, 512 * e4:512 * (e4 + 1)],
                             d_spl, deps=[t_tqw2[h]] if e4 == 0 else [])
    t_agw2_e = []
    for e4 in range(4):
        t_agw2_e.append(CC("AllGather", ALU.bypass, [list(range(8))],
                           [d_w2qin_e[e4].opt()], [d_w2q_e[e4].opt()], s_agw2,
                           deps=[t_splw2[-1]] if e4 == 0 else []))
    t_agw2 = t_agw2_e[-1]
    t_sgn = {8: t_tq1[1], 9: t_tqw2[7]}
    t_spl = {1: t_spl1[3], 9: t_splw2[7], "q1": t_tq1[1]}
    DMA(G, d_wpbf, x_wpT, d_cast)
    DMA(G, d_wobf, x_woT, d_cast)
    t_castF = Tok(G, d_cast, d_cast.n)
    t_hsel = t_csel = t_wpbf = t_wobf = t_castF

    t_if0 = OP(V, lambda e: e.tensor_copy(s_invf[:], s_iint[:]), deps=[t_iota])
    t_if1 = OP(V, lambda e: e.tensor_scalar(out=s_invf[:], in0=s_invf[:],
        scalar1=-LN1E4 / 64.0, scalar2=None, op0=ALU.mult), deps=[VD], inc=True)
    t_invf = OP(SC_, lambda e: e.activation(out=s_invf[:], in_=s_invf[:],
        func=AF.Exp), deps=[t_if1, t_small], inc=True)
    t_sin = [None] * NTILE
    t_cos = [None] * NTILE
    prev_g = None
    for t in range(NTILE):
        tf = OP(SC_, lambda e, tt=t: e.activation(out=s_iw1[:], in_=s_invf[:],
            func=AF.Identity, scale=s_posc[:, tt:tt + 1]),
            deps=[t_invf, prev_g], inc=True)
        tk = OP(V, lambda e: e.tensor_scalar(out=s_iint[:], in0=s_iw1[:],
            scalar1=1.0 / TWO_PI, scalar2=None, op0=ALU.mult),
            deps=[tf, t_sin[t - 1] if t else None])
        tk2 = OP(V, lambda e: e.tensor_copy(s_iw2[:], s_iint[:]), deps=[VD])
        OP(V, lambda e: e.scalar_tensor_tensor(out=s_iw2[:], in0=s_iw2[:],
            scalar=-TWO_PI, in1=s_iw1[:], op0=ALU.mult, op1=ALU.add),
           deps=[VD])
        OP(V, lambda e: e.tensor_scalar(out=s_iw1[:], in0=s_iw2[:],
            scalar1=PI, scalar2=None, op0=ALU.is_gt), deps=[VD])
        OP(V, lambda e: e.scalar_tensor_tensor(out=s_iw2[:], in0=s_iw1[:],
            scalar=-TWO_PI, in1=s_iw2[:], op0=ALU.mult, op1=ALU.add), deps=[VD])
        OP(V, lambda e: e.tensor_scalar(out=s_iw1[:], in0=s_iw2[:],
            scalar1=-PI, scalar2=None, op0=ALU.is_lt), deps=[VD])
        tr = OP(V, lambda e: e.scalar_tensor_tensor(out=s_iw2[:], in0=s_iw1[:],
            scalar=TWO_PI, in1=s_iw2[:], op0=ALU.mult, op1=ALU.add),
            deps=[VD], inc=True)
        t_sin[t] = OP(SC_, lambda e, tt=t: e.activation(
            out=s_sinT[:, 64 * tt:64 * (tt + 1)], in_=s_iw2[:], func=AF.Sin),
            deps=[tr] + ([t_tqw2[7]] if t == 0 else []), inc=True)
        trc = OP(V, lambda e: e.tensor_scalar(out=s_iw3[:], in0=s_iw2[:],
            scalar1=PI / 2, scalar2=None, op0=ALU.add),
            deps=[tr, t_cos[t - 1] if t else None])
        tg = OP(V, lambda e: e.tensor_scalar(out=s_iw1[:], in0=s_iw3[:],
            scalar1=PI, scalar2=None, op0=ALU.is_gt), deps=[VD])
        prev_g = OP(V, lambda e: e.scalar_tensor_tensor(out=s_iw3[:],
            in0=s_iw1[:], scalar=-TWO_PI, in1=s_iw3[:], op0=ALU.mult,
            op1=ALU.add), deps=[VD], inc=True)
        t_cos[t] = OP(SC_, lambda e, tt=t: e.activation(
            out=s_cosT[:, 64 * tt:64 * (tt + 1)], in_=s_iw3[:], func=AF.Sin),
            deps=[prev_g] + ([t_tqw2[7]] if t == 0 else []), inc=True)

    t_diag = None
    for i in range(64):
        t_diag = OP(SC_, lambda e, ii=i: e.activation(
            out=s_diag[:, 128 * ii:128 * (ii + 1)], in_=s_eye[:], func=AF.Identity,
            scale=s_cwc[:, ii:ii + 1]),
            deps=[t_small] + ([t_tqw2[7], t_splw2[7]] if i == 0 else []),
            inc=(i == 63))

    t_hmiw = None
    for m in range(16):
        t_st = None
        for h in range(16):
            idx = h * 16 + m
            t_st = OP(V, lambda e, hh=h, ii=idx: e.tensor_tensor(
                out=s_hmst[:, 128 * hh:128 * (hh + 1)], in0=s_eye[:],
                in1=s_hmb[:, ii:ii + 1].broadcast_to([128, 128]),
                op=ALU.mult),
                deps=([t_hmb, t_tqw2[7]] if h == 0 else [VD]), inc=(h == 15))
        t_hmiw = DMA(G, d_hmi[128 * m:128 * (m + 1), :], s_hmst[:], d_hmw,
                     deps=[t_st])
        sch.wait(V, d_hmw, t_hmiw.val)

    hfat = bvv(AH, 32768)
    xn1b = bvv(AXN1, 2048)
    xn1T = bvv(AXN1T, 8192)

    def w1slab(par):
        return bvv(AW1S + par * 2048, 4096)

    def w2slab(par):
        return bvv(AW2S + par * 2048, 4096)

    st = {"w1j": {}, "w2j": {}, "first": True, "pst_free": [None, None],
          "xn1_free": None, "w1l": None}

    def emit_ffn(src, dst, d_load, d_store, xl0_deps, store_tokens, g_deps=None):
        xg = fv(AXG, 8192)
        xg4 = xg.rearrange("p (t d) -> p t d", d=D)
        t_store = None
        t_gelu = {}
        t_ev2 = {}
        first = st["first"]
        for g in range(GROUPS):
            srcv = src[g * 512:(g + 1) * 512, :].rearrange("(t p) d -> p t d", p=128)
            dstv = dst[g * 512:(g + 1) * 512, :].rearrange("(t p) d -> p t d", p=128)
            xldeps = (list(xl0_deps) if g == 0 else [t_store])
            if g_deps:
                xldeps += g_deps.get(g, [])
            t_xl = DMA(SY, xg4, srcv, d_load, deps=xldeps)
            t_ss = t_sm = None
            for t4 in range(4):
                xt = xg[:, 2048 * t4:2048 * (t4 + 1)]
                t_ss = OP(SC_, lambda e, x_=xt, tt=t4: e.activation(
                    out=fv(ASCR, 2048), in_=x_, func=AF.Square,
                    accum_out=s_stat[:, 4 + tt:5 + tt]),
                    deps=[t_xl] + ([t_sgn[8]] if first and g == 0 and t4 == 0 else [SD]),
                    inc=(t4 == 3))
                t_sm = OP(V, lambda e, x_=xt, tt=t4: e.tensor_scalar(
                    out=fv(ASCR + 2048, 2048), in0=x_, scalar1=1.0, scalar2=0.0,
                    op0=ALU.mult, op1=ALU.add,
                    accum_out=s_stat[:, tt:tt + 1]), deps=[t_xl, VD],
                    inc=(t4 == 3))
            OP(V, lambda e: e.tensor_scalar(out=s_stat[:, 8:16],
                in0=s_stat[:, 0:8], scalar1=1.0 / D, scalar2=None,
                op0=ALU.mult), deps=[t_ss, t_sm])
            OP(V, lambda e: e.tensor_tensor(out=s_stat[:, 16:20],
                in0=s_stat[:, 8:12], in1=s_stat[:, 8:12], op=ALU.mult), deps=[VD])
            OP(V, lambda e: e.tensor_tensor(out=s_stat[:, 20:24],
                in0=s_stat[:, 12:16], in1=s_stat[:, 16:20], op=ALU.subtract),
               deps=[VD])
            t_ve = OP(V, lambda e: e.tensor_scalar(out=s_stat[:, 24:28],
                in0=s_stat[:, 20:24], scalar1=1e-5, scalar2=None, op0=ALU.add),
                deps=[VD], inc=True)
            t_lnv = OP(SC_, lambda e: e.activation(out=s_stat[:, 28:32],
                in_=s_stat[:, 24:28], func=AF.Ln), deps=[t_ve], inc=True)
            t_half = OP(V, lambda e: e.tensor_scalar(out=s_stat[:, 32:36],
                in0=s_stat[:, 28:32], scalar1=-0.5, scalar2=None,
                op0=ALU.mult), deps=[t_lnv], inc=True)
            t_rstd = OP(SC_, lambda e: e.activation(out=s_stat[:, 36:40],
                in_=s_stat[:, 32:36], func=AF.Exp), deps=[t_half], inc=True)
            OP(V, lambda e: e.tensor_scalar(out=s_stat[:, 40:44],
                in0=s_stat[:, 8:12], scalar1=-1.0, scalar2=None, op0=ALU.mult),
               deps=[t_rstd])
            t_cb = OP(V, lambda e: e.tensor_tensor(out=s_stat[:, 44:48],
                in0=s_stat[:, 36:40], in1=s_stat[:, 40:44], op=ALU.mult),
                deps=[VD], inc=True)
            t_evx = None
            for t4 in range(4):
                xt = xg[:, 2048 * t4:2048 * (t4 + 1)]
                t_xn1 = OP(SC_, lambda e, x_=xt, tt=t4: e.activation(
                    out=xn1b[:, :], in_=x_, func=AF.Identity,
                    scale=s_stat[:, 36 + tt:37 + tt],
                    bias=s_stat[:, 44 + tt:45 + tt]),
                    deps=[t_cb, st["xn1_free"]] +
                         ([t_spl["q1"]] if first and g == 0 and t4 == 0 else []),
                    inc=True)
                for half in range(2):
                    t_tr = None
                    for j in range(8):
                        dt_ = half * 8 + j
                        t_tr = OP(T, lambda e, hh=half, jj=j, dd=dt_: e.transpose(
                            PST[hh][:, 128 * jj:128 * (jj + 1)],
                            xn1b[:, 128 * dd:128 * (dd + 1)], s_eye[:]),
                            deps=[t_xn1, st["pst_free"][half]] if j == 0 else [],
                            inc=(j == 7))
                    ev_out = xn1T.rearrange("p (k w) -> p k w", w=512)[
                        :, 8 * half:8 * (half + 1), 128 * t4:128 * (t4 + 1)]
                    st["pst_free"][half] = t_evx = OP(
                        V, lambda e, hh=half, o=ev_out: e.tensor_copy(
                            o, PST[hh][:].rearrange("p (k w) -> p k w", w=128)),
                        deps=[t_tr], inc=True)
                st["xn1_free"] = t_evx
            t_xn1T = t_evx
            for f in range(64):
                j, j2 = f // 2, f % 2
                if j2 == 0:
                    par = j % 2
                    wdeps = [st["w1j"].get(j - 2, st["w1j"].get(j + 30))]
                    if first and j < 16:
                        if j < 2:
                            sch.wait(SY, s_agw1, t_agw1.val)
                            wdeps.append(t_spl[1])
                    elif first and j < 18:
                        sch.wait(SY, s_agw1b, t_agw1b.val)
                    w1qsrc = d_w1qA if j < 16 else d_w1qB
                    jj_ = j if j < 16 else j - 16
                    srcw = w1qsrc[:, 256 * jj_:256 * (jj_ + 1)].rearrange(
                        "(k p) w -> p k w", p=128)
                    st["w1l"] = DMA(SY, w1slab(par).rearrange(
                        "p (k w) -> p k w", w=256), srcw, d_w1[par], deps=wdeps)
                psum = PS2a if f % 2 == 0 else PS2b
                mmdeps = [st["w1l"], t_xn1T, t_gelu.get((g, f - 2))]
                if first and g == 0 and f == 0:
                    mmdeps.append(t_hmb)
                t_mm = None
                for k in range(16):
                    lhsT = w1slab(j % 2)[:, k * 256 + j2 * 128:
                                         k * 256 + j2 * 128 + 128]
                    rhs = xn1T[:, 512 * k:512 * (k + 1)]
                    t_mm = OP(T, lambda e, ps=psum, l=lhsT, r=rhs, kk=k: e.matmul(
                        ps, l, r, start=(kk == 0), stop=(kk == 15)),
                        deps=mmdeps if k == 0 else [], inc=(k == 15))
                st["w1j"][j] = t_mm
                gdeps = [t_mm]
                if first and g == 0 and f == 0:
                    gdeps += [t_sgn[8]]
                if first and g == 0 and f == 32:
                    gdeps += [t_sgn[9], t_spl[9]]
                t_gelu[(g, f)] = OP(SC_, lambda e, ps=psum, ff=f: e.activation(
                    out=hfat[:, 512 * ff:512 * (ff + 1)], in_=ps, func=AF.Gelu,
                    bias=s_b1c[:, ff:ff + 1]), deps=gdeps, inc=True)
            for e_ in range(4):
                for h2 in range(8):
                    sidx = e_ * 8 + h2
                    par = sidx % 2
                    wdeps = [st["w2j"].get(sidx - 2, st["w2j"].get(sidx + 30))]
                    if first and g == 0 and h2 == 0:
                        sch.wait(SY, s_agw2, t_agw2_e[e_].val)
                        if e_ == 0:
                            wdeps.append(t_spl[9])
                    srcw = d_w2q_e[e_][1024 * h2:1024 * (h2 + 1), :].rearrange(
                        "(k p) w -> p k w", p=128)
                    t_w2l = DMA(SY, w2slab(par).rearrange(
                        "p (k w) -> p k w", w=512), srcw, d_w2[par], deps=wdeps)
                    t_mm = None
                    for ns in range(4):
                        bank = psb(ns)
                        mmdeps = [t_w2l, t_gelu[(g, 63)]]
                        if h2 == 0:
                            mmdeps.append(
                                t_ev2.get((g, e_ - 1, ns)) if e_ > 0
                                else t_ev2.get((g - 1, 3, ns)))
                            if first and g == 0 and e_ == 0 and ns == 0:
                                mmdeps.append(t_nsb)
                        for k in range(8):
                            ft = 8 * h2 + k
                            lhsT = hfat[:, 512 * ft + 128 * ns:
                                        512 * ft + 128 * ns + 128]
                            rhs = w2slab(par)[:, 512 * k:512 * (k + 1)]
                            t_mm = OP(T, lambda e, b=bank, l=lhsT, r=rhs,
                                      stt=(h2 == 0 and k == 0): e.matmul(
                                b, l, r, start=stt, stop=False),
                                deps=mmdeps if k == 0 else [],
                                inc=(ns == 3 and k == 7))
                    st["w2j"][sidx] = t_mm
                t_mmb = None
                for ns in range(4):
                    t_mmb = OP(T, lambda e, b=psb(ns), ee=e_: e.matmul(
                        b, s_ones1[:], s_b2h[0:1, 512 * ee:512 * (ee + 1)],
                        start=False, stop=True),
                        deps=[t_b2h] if (first and g == 0 and e_ == 0 and
                                         ns == 0) else [], inc=(ns == 3))
                for ns in range(4):
                    xsl = xg[:, 2048 * ns + 512 * e_:2048 * ns + 512 * e_ + 512]
                    t_ev2[(g, e_, ns)] = OP(V, lambda e, b=psb(ns), xx=xsl:
                        e.tensor_tensor(out=xx, in0=b, in1=xx, op=ALU.add),
                        deps=[t_mmb], inc=True)
            t_store = DMA(SY, dstv, xg4, d_store, deps=[t_ev2[(g, 3, 3)]])
            store_tokens.append(t_store)
        st["first"] = False
        return t_store

    a_stores = []
    t_a_last = emit_ffn(x_x, d_x1, d_xa, d_x1w, [t_sgn[8]], a_stores)

    if stop == "A":
        t_dbg = DMA(SY, x_out, d_x1, d_out, deps=[t_a_last])
        sch.wait(SY, d_out, t_dbg.val)
        _realize(nc, sch)
        es.close()
        return nc

    xlast = fv(28800, 2048)
    t_hl = DMA(SY, xlast, d_x1[1920:2048, :], d_xh, deps=[t_a_last])

    def ln2_cols(t_xdep, xtiles, w, sc_idx, dump_off, extra_act_deps=()):
        t_ss = t_sm = None
        for t4 in range(w):
            xt = xtiles[t4]
            t_ss = OP(SC_, lambda e, x_=xt, tt=t4: e.activation(
                out=fv(dump_off, 2048), in_=x_, func=AF.Square,
                accum_out=s_stat[:, 4 + tt:5 + tt]),
                deps=[t_xdep] + list(extra_act_deps) if t4 == 0 else [SD],
                inc=(t4 == w - 1))
            t_sm = OP(V, lambda e, x_=xt, tt=t4: e.tensor_scalar(
                out=fv(dump_off + 2048, 2048), in0=x_, scalar1=1.0, scalar2=0.0,
                op0=ALU.mult, op1=ALU.add, accum_out=s_stat[:, tt:tt + 1]),
                deps=[t_xdep] if t4 == 0 else [VD], inc=(t4 == w - 1))
        OP(V, lambda e: e.tensor_scalar(out=s_stat[:, 8:16], in0=s_stat[:, 0:8],
            scalar1=1.0 / D, scalar2=None, op0=ALU.mult), deps=[t_ss, t_sm])
        OP(V, lambda e: e.tensor_tensor(out=s_stat[:, 16:20], in0=s_stat[:, 8:12],
            in1=s_stat[:, 8:12], op=ALU.mult), deps=[VD])
        OP(V, lambda e: e.tensor_tensor(out=s_stat[:, 20:24], in0=s_stat[:, 12:16],
            in1=s_stat[:, 16:20], op=ALU.subtract), deps=[VD])
        t_ve = OP(V, lambda e: e.tensor_scalar(out=s_stat[:, 24:28],
            in0=s_stat[:, 20:24], scalar1=1e-5, scalar2=None, op0=ALU.add),
            deps=[VD], inc=True)
        t_lnv = OP(SC_, lambda e: e.activation(out=s_stat[:, 28:32],
            in_=s_stat[:, 24:28], func=AF.Ln), deps=[t_ve], inc=True)
        OP(V, lambda e: e.reciprocal(s_stat[:, 32:36], s_stat[:, 24:28]),
           deps=[t_lnv])
        OP(V, lambda e: e.tensor_tensor(out=s_stat[:, 36:40], in0=s_stat[:, 20:24],
            in1=s_stat[:, 32:36], op=ALU.mult), deps=[VD])
        t_m2e = OP(V, lambda e: e.tensor_scalar(out=s_stat[:, 40:44],
            in0=s_stat[:, 36:40], scalar1=1e-6, scalar2=None, op0=ALU.add),
            deps=[VD], inc=True)
        t_ln2 = OP(SC_, lambda e: e.activation(out=s_stat[:, 44:48],
            in_=s_stat[:, 40:44], func=AF.Ln), deps=[t_m2e], inc=True)
        OP(V, lambda e: e.tensor_tensor(out=s_stat[:, 48:52], in0=s_stat[:, 28:32],
            in1=s_stat[:, 44:48], op=ALU.add), deps=[t_ln2])
        t_th = OP(V, lambda e: e.tensor_scalar(out=s_stat[:, 52:56],
            in0=s_stat[:, 48:52], scalar1=-0.5, scalar2=None, op0=ALU.mult),
            deps=[VD], inc=True)
        t_cs = OP(SC_, lambda e: e.activation(out=s_stat[:, 56:60],
            in_=s_stat[:, 52:56], func=AF.Exp), deps=[t_th], inc=True)
        t_th2 = OP(V, lambda e: e.tensor_scalar(out=s_stat[:, 60:64],
            in0=s_stat[:, 28:32], scalar1=-0.5, scalar2=None, op0=ALU.mult),
            deps=[t_cs], inc=True)
        if sc_idx is not None:
            rs_out = s_rstdA[:, 4 * sc_idx:4 * sc_idx + w]
        else:
            rs_out = s_stat[:, 60:60 + w]
        t_rstd = OP(SC_, lambda e, o=rs_out: e.activation(out=o,
            in_=s_stat[:, 60:60 + w] if sc_idx is None else s_stat[:, 60:64],
            func=AF.Exp), deps=[t_th2], inc=True)
        OP(V, lambda e: e.tensor_scalar(out=s_stat[:, 16:20], in0=s_stat[:, 8:12],
            scalar1=-1.0, scalar2=None, op0=ALU.mult), deps=[t_rstd])
        t_cb2 = OP(V, lambda e: e.tensor_tensor(out=s_stat[:, 20:24],
            in0=s_stat[:, 56:60], in1=s_stat[:, 16:20], op=ALU.mult),
            deps=[VD], inc=True)
        t_msave = None
        if sc_idx is not None:
            t_msave = OP(V, lambda e: e.tensor_copy(
                s_meanA[:, 4 * sc_idx:4 * sc_idx + w], s_stat[:, 8:8 + w]),
                deps=[VD], inc=True)
        return t_cs, t_cb2, t_rstd, t_msave

    t_cs0, t_cb20, _, _ = ln2_cols(t_hl, [xlast], 1, None, 30848)
    t_bt = OP(SC_, lambda e: e.activation(out=fv(30848, 2048), in_=xlast,
        func=AF.Identity, scale=s_stat[:, 56:57], bias=s_stat[:, 20:21]),
        deps=[t_cs0, t_cb20], inc=True)
    t_bn = OP(V, lambda e: e.tensor_tensor(out=fv(32896, 2048), in0=fv(30848, 2048),
        in1=s_nsb[:], op=ALU.mult), deps=[t_bt, t_nsb], inc=True)
    t_bc = OP(V, lambda e: e.tensor_scalar(out=bvv(34944, 2048), in0=fv(32896, 2048),
        scalar1=60000.0, scalar2=-60000.0, op0=ALU.min, op1=ALU.max),
        deps=[VD], inc=True)
    t_hi = DMA(G, d_hin, bvv(34944, 2048)[125:128, :], d_gc, deps=[t_bc])
    t_agh = CC("AllGather", ALU.bypass, [list(range(8))], [d_hin.opt()],
               [d_hout.opt()], s_agh_s, deps=[t_hi])
    t_ghl = DMA(G, agh_r, d_hout, d_gc, deps=[t_agh])
    t_selmm = None
    for dc in range(16):
        t_selmm = OP(T, lambda e, d_=dc: e.matmul(
            PSBIG[:, 3 * d_:3 * d_ + 3], agh_r[:, 128 * d_:128 * (d_ + 1)],
            s_hsel[:], start=True, stop=True),
            deps=[t_ghl, t_hsel, t_a_last] if dc == 0 else [], inc=(dc == 15))
    t_hev = OP(V, lambda e: e.tensor_copy(s_halo[:], PSBIG[:, 0:48]),
               deps=[t_selmm], inc=True)

    xn2T = bvv(BXN2T, 8240)
    xctb = bvv(BXCT, 8192)
    st2 = {"wp": {}, "conv_ps": {}, "te": {}, "silu": {}, "gspl": {},
           "vev": {}, "hm": {}, "wo": {}}
    t_dtfree = {}
    t_crlast = {}
    t_cospill = {}
    t_esspill = {}
    t_gaspill = {}
    t_telast = {}
    t_convlast = {}
    t_vseclast = {}
    t_vevlast = {}
    t_scanlast = {}
    prev_bxl_deps = [t_a_last]

    def wpslab(par):
        return bvv(BWPS + par * 2048, 4096)

    for sc in range(NSC):
        cg0 = 2 * sc
        xst = fv(0, 8192)
        xst4 = xst.rearrange("p (t d) -> p t d", d=D)
        t_bxl = DMA(SY, xst4,
                    d_x1[512 * sc:512 * (sc + 1), :].rearrange(
                        "(t p) d -> p t d", p=128), d_xb, deps=prev_bxl_deps)
        xtiles = [xst[:, 2048 * t4:2048 * (t4 + 1)] for t4 in range(4)]
        extra = [t_scanlast[sc - 1]] if sc > 0 else []
        t_cs, t_cb2, t_rstd, t_msave = ln2_cols(t_bxl, xtiles, 4, sc, BLB, extra)
        t_trlast = None
        t_evx = None
        for t4 in range(4):
            tdeps = [t_cs, t_cb2]
            if t4 > 0:
                tdeps.append(t_trlast)
            elif sc == 0:
                tdeps.append(t_hev)
            t_t = OP(SC_, lambda e, x_=xtiles[t4], tt=t4: e.activation(
                out=fv(BLB, 2048), in_=x_, func=AF.Identity,
                scale=s_stat[:, 56 + tt:57 + tt],
                bias=s_stat[:, 20 + tt:21 + tt]), deps=tdeps, inc=True)
            t_n = OP(V, lambda e: e.tensor_tensor(out=fv(BLB + 2048, 2048),
                in0=fv(BLB, 2048), in1=s_nsb[:], op=ALU.mult),
                deps=[t_t], inc=True)
            t_c = OP(V, lambda e: e.tensor_scalar(out=bvv(BLB, 2048),
                in0=fv(BLB + 2048, 2048), scalar1=60000.0, scalar2=-60000.0,
                op0=ALU.min, op1=ALU.max), deps=[VD], inc=True)
            for half in range(2):
                t_tr = None
                for j in range(8):
                    dt_ = half * 8 + j
                    t_tr = OP(T, lambda e, hh=half, jj=j, dd=dt_: e.transpose(
                        PST[hh][:, 128 * jj:128 * (jj + 1)],
                        bvv(BLB, 2048)[:, 128 * dd:128 * (dd + 1)], s_eye[:]),
                        deps=[t_c, st["pst_free"][half]] if j == 0 else [],
                        inc=(j == 7))
                ev_out = xn2T.rearrange("p (k w) -> p k w", w=515)[
                    :, 8 * half:8 * (half + 1), 3 + 128 * t4:3 + 128 * (t4 + 1)]
                evdeps = [t_tr]
                if sc == 0 and t4 == 0 and half == 0:
                    evdeps.append(t_hev)
                if sc > 0 and t4 == 0 and half == 0:
                    evdeps.append(t_scanlast[sc - 1])
                st["pst_free"][half] = t_evx = OP(
                    V, lambda e, hh=half, o=ev_out: e.tensor_copy(
                        o, PST[hh][:].rearrange("p (k w) -> p k w", w=128)),
                    deps=evdeps, inc=True)
            t_trlast = t_tr
        t_hw = OP(V, lambda e: e.tensor_copy(
            xn2T.rearrange("p (k w) -> p k w", w=515)[:, :, 0:3],
            s_halo[:].rearrange("p (k w) -> p k w", w=3)),
            deps=[t_evx, t_hev], inc=True)
        t_hsv = OP(V, lambda e: e.tensor_copy(
            s_halo[:].rearrange("p (k w) -> p k w", w=3),
            xn2T.rearrange("p (k w) -> p k w", w=515)[:, :, 512:515]),
            deps=[VD], inc=True)
        t_silu = {}
        t_convmm = {}
        for dt_ in range(DTL):
            psum = PS2a if dt_ % 2 == 0 else PS2b
            t_mm = None
            for k in range(4):
                cdeps = []
                if k == 0:
                    cdeps = [t_hw, t_hsv, t_diag, t_silu.get(dt_ - 2)]
                    if sc == 0 and dt_ < 2:
                        cdeps.append(t_a_last)
                    if sc > 0 and dt_ < 2:
                        cdeps.append(t_telast[sc - 1])
                t_mm = OP(T, lambda e, ps=psum, dd=dt_, kk=k: e.matmul(
                    ps, s_diag[:, 128 * (4 * dd + kk):128 * (4 * dd + kk + 1)],
                    xn2T[:, 515 * dd + kk:515 * dd + kk + 512],
                    start=(kk == 0), stop=(kk == 3)),
                    deps=cdeps, inc=(k == 3))
            t_convmm[dt_] = t_mm
            sdeps = [t_mm]
            if sc == 0 and dt_ == 0:
                sdeps.append(t_hev)
            if sc > 0 and dt_ == 0:
                sdeps.append(t_scanlast[sc - 1])
            t_silu[dt_] = OP(SC_, lambda e, ps=psum, dd=dt_: e.activation(
                out=xctb.rearrange("p (c w) -> p c w", w=4096)[
                    :, :, 256 * dd:256 * (dd + 1)],
                in_=ps.rearrange("p (c w) -> p c w", w=256),
                func=AF.Silu, bias=s_cbc[:, dd:dd + 1]), deps=sdeps, inc=True)
        t_convlast[sc] = t_convmm[15]
        t_silulast = t_silu[15]
        dtv = fv(BDT0, 8192).rearrange("p (c w) -> p c w", w=4096)
        t_te = {}
        t_dtc = {}
        for o in range(DTL):
            si = o // 2
            if o % 2 == 0:
                par = si % 2
                wdeps = [st2["wp"].get(si - 2)]
                if sc == 0 and si < 2:
                    wdeps += [t_wpbf, t_hev, t_hi]
                if sc > 0 and si < 2:
                    wdeps.append(st2["wp"].get(si + 22))
                st2["wpl"] = DMA(SY, wpslab(par).rearrange(
                    "p (k w) -> p k w", w=256),
                    d_wpbf[:, 256 * si:256 * (si + 1)].rearrange(
                        "(k p) w -> p k w", p=128), d_wp[par], deps=wdeps)
            pbase = 1024 * (o % 2)
            psum2 = PSBIG[:, pbase:pbase + 1024].rearrange(
                "p (c w) -> p c w", w=512)[:, :, 0:256]
            t_mm = None
            for k in range(16):
                mdeps = []
                if k == 0:
                    mdeps = [st2["wpl"], t_silulast, t_te.get(o - 2)]
                    if o < 2:
                        mdeps.append(t_vevlast[sc - 1] if sc > 0 else t_hev)
                for cpar in range(2):
                    t_mm = OP(T, lambda e, oo=o, kk=k, cc_=cpar,
                              pp=par, pb=pbase: e.matmul(
                        PSBIG[:, pb + 512 * cc_:pb + 512 * cc_ + 256],
                        wpslab(pp)[:, 256 * kk + 128 * (oo % 2):
                                   256 * kk + 128 * (oo % 2) + 128],
                        xctb[:, 4096 * cc_ + 256 * kk:4096 * cc_ + 256 * kk + 256],
                        start=(kk == 0), stop=(kk == 15)),
                        deps=mdeps if (k == 0 and cpar == 0) else [],
                        inc=(k == 15 and cpar == 1))
            st2["wp"][si] = t_mm
            tedeps = [t_mm, SD]
            if o == 0:
                tedeps.append(t_crlast[sc - 1] if sc > 0 else None)
            t_te[o] = OP(SC_, lambda e, ps=psum2, oo=o: e.activation(
                out=fv(BCREP, 512).rearrange("p (c w) -> p c w", w=256),
                in_=ps, func=AF.Exp,
                bias=s_dtbc[:, oo:oo + 1]), deps=tedeps, inc=True)
            t_tl = OP(SC_, lambda e: e.activation(out=fv(BCREP + 512, 512),
                in_=fv(BCREP, 512), func=AF.Ln, bias=1.0),
                deps=[SD, t_dtc.get(o - 1)], inc=True)
            t_dtc[o] = OP(V, lambda e, oo=o: e.tensor_scalar(
                out=dtv[:, :, 256 * oo:256 * (oo + 1)],
                in0=fv(BCREP + 512, 512).rearrange("p (c w) -> p c w", w=256),
                scalar1=0.001, scalar2=2.0, op0=ALU.max, op1=ALU.min),
                deps=[t_tl] + ([t_t] if o == 0 else []), inc=True)
        t_telast[sc] = t_te[15]
        t_ropelast = None
        part1_tok = {}

        def scan_part1(cpar, first_deps):
            cg = cg0 + cpar
            dtc = fv(BDT0 + cpar * 4096, 4096)
            LB = fv(BLB, 4096)
            SA = fv(BSA, 4096) if cpar == 0 else fv(BXCT, 4096)
            esf = bvv(BES + cpar * 2048, 4096)
            t_sc = None
            for dt_ in range(DTL):
                scdeps = [VD] if dt_ else ([t_one, t_dtc[15]] + first_deps)
                t_sc = OP(V, lambda e, dd=dt_, L_=LB, d_=dtc: e.tensor_tensor_scan(
                    out=L_[:, 256 * dd:256 * (dd + 1)],
                    data0=ones_ap, data1=d_[:, 256 * dd:256 * (dd + 1)],
                    initial=0.0, op0=ALU.mult, op1=ALU.add), deps=scdeps)
            t_L = OP(V, lambda e, L_=LB: e.tensor_scalar(out=L_[:, 0:4096],
                in0=L_[:, 0:4096], scalar1=-1.0, scalar2=-20.0,
                op0=ALU.mult, op1=ALU.max), deps=[VD], inc=True)
            OP(V, lambda e, cc_=cg, L_=LB: e.tensor_copy(
                s_lmA[:, 16 * cc_:16 * (cc_ + 1)], L_[:, 0::256]), deps=[VD])
            OP(V, lambda e, cc_=cg: e.tensor_scalar(
                out=s_negl[:, 0:16] if cc_ % 2 == 0 else s_hcol_negl,
                in0=s_lmA[:, 16 * cc_:16 * (cc_ + 1)], scalar1=-1.0,
                scalar2=None, op0=ALU.mult), deps=[VD])
            t_bd = OP(V, lambda e, cc_=cg, L_=LB: e.tensor_copy(
                s_bdA[:, cc_::8], L_[:, 255::256]), deps=[VD], inc=True)
            part1_tok[("bd", cpar)] = t_bd
            return t_bd

        def scan_part1b(cpar):
            cg = cg0 + cpar
            LB = fv(BLB, 4096)
            SA = fv(BSA, 4096) if cpar == 0 else fv(BXCT, 4096)
            esf = bvv(BES + cpar * 2048, 4096)
            t_bd = part1_tok[("bd", cpar)]
            t_en = None
            for dt_ in range(DTL):
                endeps = [t_bd] if dt_ == 0 else [SD]
                if dt_ == 0:
                    endeps.append(t_vseclast[sc] if cpar else t_convlast[sc])
                t_en = OP(SC_, lambda e, dd=dt_, A_=SA, L_=LB: e.activation(
                    out=A_[:, 256 * dd:256 * (dd + 1)],
                    in_=L_[:, 256 * dd:256 * (dd + 1)], func=AF.Exp, scale=-1.0,
                    bias=L_[:, 256 * dd:256 * dd + 1]), deps=endeps,
                    inc=(dt_ == 15))
            t_es = None
            for dt_ in range(DTL):
                esdeps = [SD]
                if dt_ == 0 and sc > 0:
                    esdeps.append(t_esspill[(sc - 1, 1)])
                nglap = s_negl[:, 0:16] if cpar == 0 else s_hcol_negl
                t_es = OP(SC_, lambda e, dd=dt_, E_=esf, L_=LB, N_=nglap:
                    e.activation(
                    out=E_[:, 256 * dd:256 * (dd + 1)],
                    in_=L_[:, 256 * dd:256 * (dd + 1)], func=AF.Exp,
                    bias=N_[:, dd:dd + 1]), deps=esdeps, inc=(dt_ == 15))
            part1_tok[cpar] = (t_en, t_es)
            return t_es

        def scan_part2(cpar):
            cg = cg0 + cpar
            dtc = fv(BDT0 + cpar * 4096, 4096)
            SA = fv(BSA, 4096) if cpar == 0 else fv(BXCT, 4096)
            SB = fv(BSB, 4096) if cpar == 0 else fv(BLB, 4096)
            esf = bvv(BES + cpar * 2048, 4096)
            cof = bvv(BCO + cpar * 2048, 4096)
            vrf = bvv(BVROT + cpar * 2048, 4096)
            t_en, t_es = part1_tok[cpar]
            t_A = OP(V, lambda e, B_=SB, d_=dtc, A_=SA: e.tensor_tensor(
                out=B_[:, 0:4096],
                in0=d_, in1=A_[:, 0:4096], op=ALU.mult),
                deps=[t_en, t_es], inc=True)
            t_u = OP(V, lambda e, A_=SA, B_=SB, v_=vrf: e.tensor_tensor(
                out=A_[:, 0:4096],
                in0=B_[:, 0:4096], in1=v_, op=ALU.mult), deps=[VD], inc=True)
            t_cum = None
            for dt_ in range(DTL):
                t_cum = OP(V, lambda e, dd=dt_, A_=SA, B_=SB: e.tensor_tensor_scan(
                    out=B_[:, 256 * dd:256 * (dd + 1)],
                    data0=ones_ap, data1=A_[:, 256 * dd:256 * (dd + 1)],
                    initial=0.0, op0=ALU.mult, op1=ALU.add), deps=[VD])
            codeps = [VD, t_es]
            if sc > 0:
                codeps.append(t_esspill[(sc - 1, 1)])
            t_co = OP(V, lambda e, C_=cof, B_=SB, E_=esf: e.tensor_tensor(
                out=C_, in0=B_[:, 0:4096],
                in1=E_, op=ALU.mult), deps=codeps, inc=True)
            t_bo = OP(V, lambda e, cc_=cg, C_=cof: e.tensor_copy(
                s_boA[:, cc_::8], C_[:, 255::256]), deps=[VD], inc=True)
            t_cospill[(sc, cpar)] = DMA(G, d_co[128 * cg:128 * (cg + 1), :],
                                        cof, d_cosp,
                                        deps=[t_bo])
            t_esspill[(sc, cpar)] = DMA(G, d_es[128 * cg:128 * (cg + 1), :],
                                        esf, d_cosp, deps=[t_co])
            return t_A, t_co, t_es

        ones_ap = s_one[:]
        scan_part1(0, [t_telast[sc], t_trlast])

        t_gs = {}
        for o in range(DTL):
            si = 8 + o // 2
            if o % 2 == 0:
                par = si % 2
                st2["wpl"] = DMA(SY, wpslab(par).rearrange(
                    "p (k w) -> p k w", w=256),
                    d_wpbf[:, 4096 + 256 * (o // 2):4096 + 256 * (o // 2 + 1)
                           ].rearrange("(k p) w -> p k w", p=128), d_wp[par],
                    deps=[st2["wp"].get(si - 2)])
            pbase = 1024 * (o % 2)
            psum2 = PSBIG[:, pbase:pbase + 1024].rearrange(
                "p (c w) -> p c w", w=512)[:, :, 0:256]
            t_mm = None
            for k in range(16):
                mdeps = []
                if k == 0:
                    mdeps = [st2["wpl"], t_gs.get(o - 2, t_te.get(14 + o))]
                for cpar in range(2):
                    t_mm = OP(T, lambda e, oo=o, kk=k, cc_=cpar,
                              pp=par, pb=pbase: e.matmul(
                        PSBIG[:, pb + 512 * cc_:pb + 512 * cc_ + 256],
                        wpslab(pp)[:, 256 * kk + 128 * (oo % 2):
                                   256 * kk + 128 * (oo % 2) + 128],
                        xctb[:, 4096 * cc_ + 256 * kk:4096 * cc_ + 256 * kk + 256],
                        start=(kk == 0), stop=(kk == 15)),
                        deps=mdeps if (k == 0 and cpar == 0) else [],
                        inc=(k == 15 and cpar == 1))
            st2["wp"][si] = t_mm
            gping = bvv(BGA + (o % 2) * 256, 512)
            t_gs[o] = OP(SC_, lambda e, ps=psum2, gp=gping: e.activation(
                out=gp.rearrange("p (c w) -> p c w", w=256),
                in_=ps, func=AF.Sigmoid),
                deps=[t_mm, st2["gspl"].get(o - 2, st2["gspl"].get(o + 14))],
                inc=True)
            st2["gspl"][o] = DMA(G, d_ga.rearrange("(c p) w -> p c w", p=128)[
                :, cg0:cg0 + 2, 256 * o:256 * (o + 1)],
                gping.rearrange("p (c w) -> p c w", w=256), d_gasp[o % 2],
                deps=[t_gs[o]])
        t_gaspill[sc] = st2["gspl"][15]
        t_gslast = t_gs[15]
        st2["gslast"] = (t_gs[14], t_gs[15])
        t_es0 = scan_part1b(0)

        t_vev = {}
        for dv in range(4):
            for kh in range(2):
                si = 16 + dv * 2 + kh
                par = si % 2
                st2["wpl"] = DMA(SY, wpslab(par).rearrange(
                    "p (k w) -> p k w", w=512),
                    d_wpbf[1024 * kh:1024 * (kh + 1),
                           2048 + 512 * dv:2048 + 512 * (dv + 1)].rearrange(
                        "(k p) w -> p k w", p=128), d_wp[par],
                    deps=[st2["wp"].get(si - 2)])
                t_mm = None
                for nt in range(4):
                    bank = psb(nt)
                    mdeps = []
                    if kh == 0 and nt == 0:
                        mdeps = [st2["wpl"], t_vev.get((dv - 1, 3))]
                        if dv == 0:
                            mdeps += [t_gs[15], t_gs[14]]
                    elif nt == 0:
                        mdeps = [st2["wpl"]]
                    for k in range(8):
                        ft = 8 * kh + k
                        t_mm = OP(T, lambda e, b=bank, nn=nt, f_=ft, kk=k,
                                  pp=par: e.matmul(
                            b, xctb[:, 4096 * (nn // 2) + 256 * f_ +
                                    128 * (nn % 2):4096 * (nn // 2) +
                                    256 * f_ + 128 * (nn % 2) + 128],
                            wpslab(pp)[:, 512 * kk:512 * (kk + 1)],
                            start=(f_ == 0), stop=(f_ == 15)),
                            deps=mdeps if k == 0 else [],
                            inc=(k == 7 and nt == 3))
                st2["wp"][si] = t_mm
            for nt in range(4):
                vdeps = [t_mm]
                if sc > 0 and dv == 0 and nt == 0:
                    vdeps += [t_esspill[(sc - 1, 1)]]
                if sc == 0 and dv == 0 and nt == 0:
                    vdeps += [t_hi]
                t_vev[(dv, nt)] = OP(V, lambda e, b=psb(nt), nn=nt, dd=dv:
                    e.tensor_copy(bvv(BVV4 + nn * 1024, 2048)[
                        :, 512 * dd:512 * (dd + 1)], b), deps=vdeps, inc=True)
        t_vseclast[sc] = st2["wp"][23]
        t_vevlast[sc] = t_vev[(3, 3)]
        t_vrot_ev = {}
        for nt in range(4):
            gnt = sc * 4 + nt
            cpar = nt // 2
            t_cr = OP(V, lambda e, gg=gnt: e.tensor_copy(
                bvv(BCREP, 1024).rearrange("p (h w) -> p h w", w=64),
                s_cosT[:, 64 * gg:64 * (gg + 1)].unsqueeze(1).broadcast_to(
                    [128, 16, 64])),
                deps=[t_cos[15], t_vev[(3, nt)], t_dtc[15]], inc=True)
            t_sr = OP(V, lambda e, gg=gnt: e.tensor_copy(
                bvv(BCREP + 512, 1024).rearrange("p (h w) -> p h w", w=64),
                s_sinT[:, 64 * gg:64 * (gg + 1)].unsqueeze(1).broadcast_to(
                    [128, 16, 64])), deps=[t_sin[15], VD], inc=True)
            vt3 = bvv(BVV4 + nt * 1024, 2048).rearrange("p (h w) -> p h w", w=128)
            vr3 = bvv(BVR, 2048).rearrange("p (h w) -> p h w", w=128)
            cr3 = bvv(BCREP, 1024).rearrange("p (h w) -> p h w", w=64)
            sr3 = bvv(BCREP + 512, 1024).rearrange("p (h w) -> p h w", w=64)
            S1 = s_hmst[:, 0:1024].rearrange("p (h w) -> p h w", w=64)
            S2 = s_hmst[:, 1024:2048].rearrange("p (h w) -> p h w", w=64)
            rope_deps = [t_cr, t_sr]
            if sc == 0 and nt == 0:
                rope_deps.append(t_hmiw)
            if nt > 0:
                rope_deps.append(t_vrot_ev[(nt - 1, 1)])
            OP(V, lambda e, v_=vt3, r_=vr3, c_=cr3: e.tensor_tensor(
                out=r_[:, :, 0:64],
                in0=v_[:, :, 0:64], in1=c_, op=ALU.mult), deps=rope_deps)
            OP(V, lambda e, v_=vt3, r_=vr3, s_=sr3: e.tensor_tensor(
                out=r_[:, :, 64:128],
                in0=v_[:, :, 0:64], in1=s_, op=ALU.mult), deps=[VD])
            OP(V, lambda e, v_=vt3, s_=sr3, o_=S1: e.tensor_tensor(
                out=o_, in0=v_[:, :, 64:128],
                in1=s_, op=ALU.mult), deps=[VD])
            OP(V, lambda e, v_=vt3, c_=cr3, o_=S2: e.tensor_tensor(
                out=o_, in0=v_[:, :, 64:128],
                in1=c_, op=ALU.mult), deps=[VD])
            OP(V, lambda e, r_=vr3, o_=S1: e.tensor_tensor(out=r_[:, :, 0:64],
                in0=r_[:, :, 0:64], in1=o_, op=ALU.subtract), deps=[VD])
            t_rope = t_ropelast = OP(V, lambda e, r_=vr3, o_=S2: e.tensor_tensor(
                out=r_[:, :, 64:128],
                in0=r_[:, :, 64:128], in1=o_, op=ALU.add), deps=[VD], inc=True)
            for half in range(2):
                t_tr = None
                for j in range(8):
                    dt_ = half * 8 + j
                    t_tr = OP(T, lambda e, hh=half, jj=j, dd=dt_: e.transpose(
                        PST[hh][:, 128 * jj:128 * (jj + 1)],
                        bvv(BVR, 2048)[:, 128 * dd:128 * (dd + 1)], s_eye[:]),
                        deps=[t_rope, st["pst_free"][half]] if j == 0 else [],
                        inc=(j == 7))
                ev_out = bvv(BVROT + cpar * 2048, 4096).rearrange(
                    "p (k w) -> p k w", w=256)[:, 8 * half:8 * (half + 1),
                                               128 * (nt % 2):128 * (nt % 2) + 128]
                evdeps = [t_tr]
                if sc > 0 and nt == 0 and half == 0:
                    evdeps.append(t_scanlast[sc - 1])
                st["pst_free"][half] = t_vrot_ev[(nt, half)] = OP(
                    V, lambda e, hh=half, o=ev_out: e.tensor_copy(
                        o, PST[hh][:].rearrange("p (k w) -> p k w", w=128)),
                    deps=evdeps, inc=True)
        t_A0, t_co0, _ = scan_part2(0)
        scan_part1(1, [t_es0])
        scan_part1b(1)
        t_A1, t_co1, t_es1 = scan_part2(1)
        t_dtfree[sc] = t_A1
        t_scanlast[sc] = t_co1
        t_crlast[sc] = t_ropelast
        prev_bxl_deps = [t_dtfree[sc]]
        prev_bxl_deps = [t_dtfree[sc]]

    if stop == "B1":
        t_dbg = DMA(G, x_out.rearrange("(r s) d -> r (s d)", s=2), d_co, d_out,
                    deps=[t_esspill[(3, 1)]])
        sch.wait(G, d_out, t_dbg.val)
        _realize(nc, sch)
        es.close()
        return nc

    c32a = AR[0:32, 0:2048]
    c32b = AR[0:32, 2048:4096]
    c32c = AR[0:32, 4096:6144]
    c32e = AR[0:32, 6144:8192]
    c32h = ARB[0:32, 2 * 8192:2 * 8192 + 4096]
    c8 = AR[0:8, 10240:12288]
    c8hb = ARB[0:8, 2 * 12288:2 * 12288 + 2048]
    t_k0 = OP(V, lambda e: e.tensor_copy(s_bdh[:, 0:128], s_bdA[:]),
              deps=[t_dtfree[3], t_scanlast[3]], inc=True)
    OP(V, lambda e: e.tensor_tensor(out=s_bdh[:, 128:256], in0=s_bdA[:],
        in1=s_bdh[:, 0:128], op=ALU.subtract), deps=[VD])
    t_k1 = OP(V, lambda e: e.tensor_copy(s_boh[:], s_boA[:]), deps=[VD], inc=True)
    t_mm = None
    for dt_ in range(DTL):
        OP(T, lambda e, dd=dt_: e.matmul(PSBIG[0:8, 128 * dd:128 * (dd + 1)],
            s_bdh[:, 8 * dd:8 * (dd + 1)], s_eye[:], start=True, stop=False),
           deps=[t_k1, t_vevlast[3]] if dt_ == 0 else [])
        t_mm = OP(T, lambda e, dd=dt_: e.matmul(
            PSBIG[0:8, 128 * dd:128 * (dd + 1)],
            s_bdh[:, 128 + 8 * dd:128 + 8 * (dd + 1)], s_eye[:],
            start=False, stop=True), inc=(dt_ == 15))
    t_bde = OP(V, lambda e: e.tensor_copy(c8, PSBIG[0:8, :]),
               deps=[t_mm, t_scanlast[3]], inc=True)
    t_bdw = DMA(G, d_cin[0:8, :], c8, d_cbn, deps=[t_bde])
    t_mm = None
    for dt_ in range(DTL):
        t_mm = OP(T, lambda e, dd=dt_: e.matmul(
            PSBIG[0:8, 128 * dd:128 * (dd + 1)],
            s_boh[:, 8 * dd:8 * (dd + 1)], s_eye[:], start=True, stop=True),
            deps=[t_bde] if dt_ == 0 else [], inc=(dt_ == 15))
    t_boe = OP(V, lambda e: e.tensor_copy(c8, PSBIG[0:8, :]),
               deps=[t_mm, t_bdw], inc=True)
    t_bow = DMA(G, d_cin[8:16, :], c8, d_cbn, deps=[t_boe])
    t_agc = CC("AllGather", ALU.bypass, [[0, 1, 2, 3], [4, 5, 6, 7]],
               [d_cin.opt()], [d_cout.opt()], s_agc_s, deps=[t_bdw, t_bow])
    cview = d_cout.rearrange("(r k) d -> r k d", k=16)
    t_bdg = DMA(SY, c32a, cview[:, 0:8, :], d_cbg, deps=[t_agc, t_dtfree[3]])
    t_bog = DMA(SY, c32b, cview[:, 8:16, :], d_cbg, deps=[t_agc])
    t_h0 = OP(V, lambda e: e.tensor_copy(c32h[:, 0:2048], c32a),
              deps=[t_bog, t_boe], inc=True)
    t_h1 = OP(V, lambda e: e.tensor_tensor(out=c32h[:, 2048:4096], in0=c32a,
        in1=c32h[:, 0:2048], op=ALU.subtract), deps=[VD], inc=True)
    t_mm = None
    for q in range(4):
        OP(T, lambda e, qq=q: e.matmul(PSBIG[0:32, 512 * qq:512 * (qq + 1)],
            s_tri[:], c32h[:, 512 * qq:512 * (qq + 1)], start=True, stop=False),
           deps=[t_h1, t_boe] if q == 0 else [])
        t_mm = OP(T, lambda e, qq=q: e.matmul(
            PSBIG[0:32, 512 * qq:512 * (qq + 1)], s_tri[:],
            c32h[:, 2048 + 512 * qq:2048 + 512 * (qq + 1)], start=False,
            stop=True), inc=(q == 3))
    t_cd = OP(V, lambda e: e.tensor_scalar(out=c32a, in0=PSBIG[0:32, :],
        scalar1=-80.0, scalar2=0.0, op0=ALU.max, op1=ALU.min),
        deps=[t_mm], inc=True)
    t_stab = OP(V, lambda e: e.stream_shuffle(c32c, c32a, mask=[0] * 32),
                deps=[VD])
    OP(V, lambda e: e.tensor_tensor(out=c32e, in0=c32a, in1=c32c,
        op=ALU.subtract), deps=[VD])
    t_ncd = OP(V, lambda e: e.tensor_scalar(out=c32e, in0=c32e, scalar1=-20.0,
        scalar2=0.0, op0=ALU.max, op1=ALU.min), deps=[VD], inc=True)
    t_eneg = OP(SC_, lambda e: e.activation(out=c32c, in_=c32e, func=AF.Exp,
        scale=-1.0), deps=[t_ncd], inc=True)
    t_epos = OP(SC_, lambda e: e.activation(out=c32a, in_=c32e, func=AF.Exp),
                deps=[SD], inc=True)
    t_seed = OP(V, lambda e: e.tensor_tensor(out=c32e, in0=c32b, in1=c32c,
        op=ALU.mult), deps=[t_epos, t_bog], inc=True)
    OP(V, lambda e: e.tensor_copy(c32h[:, 0:2048], c32e), deps=[VD])
    t_h2 = OP(V, lambda e: e.tensor_tensor(out=c32h[:, 2048:4096], in0=c32e,
        in1=c32h[:, 0:2048], op=ALU.subtract), deps=[VD], inc=True)
    t_mm = None
    for q in range(4):
        OP(T, lambda e, qq=q: e.matmul(PSBIG[0:32, 512 * qq:512 * (qq + 1)],
            s_stri[:], c32h[:, 512 * qq:512 * (qq + 1)], start=True,
            stop=False), deps=[t_h2, t_cd] if q == 0 else [])
        t_mm = OP(T, lambda e, qq=q: e.matmul(
            PSBIG[0:32, 512 * qq:512 * (qq + 1)], s_stri[:],
            c32h[:, 2048 + 512 * qq:2048 + 512 * (qq + 1)], start=False,
            stop=True), inc=(q == 3))
    t_car = OP(V, lambda e: e.tensor_tensor(out=c32c, in0=PSBIG[0:32, :],
        in1=c32a, op=ALU.mult), deps=[t_mm, t_epos], inc=True)
    OP(V, lambda e: e.tensor_copy(c32h[:, 0:2048], c32c), deps=[VD])
    t_h3 = OP(V, lambda e: e.tensor_tensor(out=c32h[:, 2048:4096], in0=c32c,
        in1=c32h[:, 0:2048], op=ALU.subtract), deps=[VD], inc=True)
    t_mm = None
    for q in range(4):
        OP(T, lambda e, qq=q: e.matmul(PSBIG[0:8, 512 * qq:512 * (qq + 1)],
            s_csel[:], c32h[:, 512 * qq:512 * (qq + 1)], start=True,
            stop=False), deps=[t_h3, t_csel, t_car] if q == 0 else [])
        t_mm = OP(T, lambda e, qq=q: e.matmul(
            PSBIG[0:8, 512 * qq:512 * (qq + 1)], s_csel[:],
            c32h[:, 2048 + 512 * qq:2048 + 512 * (qq + 1)], start=False,
            stop=True), inc=(q == 3))
    t_c8e = OP(V, lambda e: e.tensor_copy(c8, PSBIG[0:8, :]),
               deps=[t_mm, t_car], inc=True)
    t_c8h = OP(V, lambda e: e.tensor_copy(c8hb, c8), deps=[VD], inc=True)
    t_mm = None
    for dt_ in range(DTL):
        t_mm = OP(T, lambda e, dd=dt_: e.matmul(PS2b[:, 8 * dd:8 * (dd + 1)],
            c8hb[:, 128 * dd:128 * (dd + 1)], s_eye[0:8, 0:8], start=True,
            stop=True), deps=[t_c8h, st2["gslast"][1]] if dt_ == 0 else [],
            inc=(dt_ == 15))
    t_corr0 = OP(V, lambda e: e.tensor_copy(
        s_corr[:].rearrange("p (c t) -> p t c", t=16),
        PS2b[:, 0:128].rearrange("p (t c) -> p t c", c=8)), deps=[t_mm],
        inc=True)
    t_elm = OP(SC_, lambda e: e.activation(out=s_cscr[:], in_=s_lmA[:],
        func=AF.Exp), deps=[t_epos, t_scanlast[3]], inc=True)
    t_corrF = OP(V, lambda e: e.tensor_tensor(out=s_corr[:], in0=s_corr[:],
        in1=s_cscr[:], op=ALU.mult), deps=[t_corr0, t_elm], inc=True)

    t_x2store = {}
    t_opev = {}
    t_hmlast = None
    t_oplast = None
    t_oplastprev = None
    t_finprev = None
    for sc in range(NSC):
        cg0 = 2 * sc
        t_spF = Tok(G, d_cosp, d_cosp.n)
        rdeps0 = [t_corrF, t_c8e, t_spF]
        if sc > 0:
            rdeps0 = [t_finprev, t_spF]
        t_rco = DMA(SY, bvv(CCO, 8192).rearrange("p (c w) -> p c w", w=4096),
                    d_co.rearrange("(c p) w -> p c w", p=128)[:, cg0:cg0 + 2, :],
                    d_rlA, deps=rdeps0)
        t_res = DMA(SY, bvv(CES, 8192).rearrange("p (c w) -> p c w", w=4096),
                    d_es.rearrange("(c p) w -> p c w", p=128)[:, cg0:cg0 + 2, :],
                    d_rlA, deps=[t_corrF])
        gadeps = [Tok(G, d_gasp[0], d_gasp[0].n), Tok(G, d_gasp[1], d_gasp[1].n)]
        if sc > 0:
            gadeps.append(t_hmlast)
        t_rga = DMA(SY, bvv(CGA, 8192).rearrange("p (c w) -> p c w", w=4096),
                    d_ga.rearrange("(c p) w -> p c w", p=128)[:, cg0:cg0 + 2, :],
                    d_rlB, deps=gadeps)
        t_rx1 = DMA(SY, fv(CX1, 8192).rearrange("p (t d) -> p t d", d=D),
                    d_x1[512 * sc:512 * (sc + 1), :].rearrange(
                        "(t p) d -> p t d", p=128), d_rlC,
                    deps=([t_scanlast[3], t_esspill[(3, 0)], t_esspill[(3, 1)]]
                          if sc == 0 else [t_x2store[sc - 1]]))
        t_fin = {}
        for cpar in range(2):
            cg = cg0 + cpar
            t_f = None
            for dt_ in range(DTL):
                fdeps = []
                if dt_ == 0:
                    fdeps = [t_res, t_corrF]
                    if sc > 0:
                        fdeps.append(t_oplastprev)
                t_f = OP(V, lambda e, cc_=cpar, dd=dt_, g_=cg:
                    e.scalar_tensor_tensor(
                        out=bvv(CFIN + cc_ * 2048, 4096)[:, 256 * dd:256 * (dd + 1)],
                        in0=bvv(CES + cc_ * 2048, 4096)[:, 256 * dd:256 * (dd + 1)],
                        scalar=s_corr[:, 16 * g_ + dd:16 * g_ + dd + 1],
                        in1=bvv(CCO + cc_ * 2048, 4096)[:, 256 * dd:256 * (dd + 1)],
                        op0=ALU.mult, op1=ALU.add), deps=fdeps if dt_ == 0 else [VD],
                    inc=(dt_ == 15))
            t_fin[cpar] = t_f
        t_finprev = t_fin[1]
        t_vmg = {}
        for m in range(16):
            hmdeps = [st2["hm"].get(m - 2, st2["hm"].get(14) if sc > 0 else None)]
            if sc == 0 and m == 0:
                hmdeps.append(t_hmiw)
            t_hml = DMA(SY, bvv(CHM + (m % 2) * 1024, 2048),
                        d_hmi[128 * m:128 * (m + 1), :], d_hmd[m % 2],
                        deps=hmdeps)
            for cpar in range(2):
                psum = PS2a if cpar == 0 else PS2b
                t_mm = None
                for h in range(16):
                    mdeps = []
                    if h == 0:
                        mdeps = [t_hml, t_fin[cpar], t_vmg.get((cpar, m - 1))]
                        if sc == 0 and m == 0:
                            mdeps += [t_corr0, st2["gslast"][cpar]]
                    t_mm = OP(T, lambda e, ps=psum, hh=h, mm_=m, cc_=cpar:
                        e.matmul(ps[:, 0:256],
                                 bvv(CHM + (mm_ % 2) * 1024, 2048)[
                                     :, 128 * hh:128 * (hh + 1)],
                                 bvv(CFIN + cc_ * 2048, 4096)[
                                     :, 256 * hh:256 * (hh + 1)],
                                 start=(hh == 0), stop=(hh == 15)),
                        deps=mdeps if h == 0 else [], inc=(h == 15))
                t_vmg[(cpar, m)] = OP(
                    V, lambda e, ps=psum, mm_=m, cc_=cpar: e.tensor_tensor(
                        out=bvv(CVMG + cc_ * 2048, 4096)[:, 256 * mm_:256 * (mm_ + 1)],
                        in0=ps[:, 0:256],
                        in1=bvv(CGA + cc_ * 2048, 4096)[:, 256 * mm_:256 * (mm_ + 1)],
                        op=ALU.mult), deps=[t_mm, t_rga], inc=True)
            st2["hm"][m] = t_vmg[(1, m)]
        t_hmlast = t_vmg[(1, 15)]
        OP(V, lambda e, s_=sc: e.tensor_scalar(out=s_stat[:, 0:4],
            in0=s_rstdA[:, 4 * s_:4 * s_ + 4], scalar1=1.0, scalar2=None,
            op0=ALU.add), deps=[t_rx1])
        OP(V, lambda e, s_=sc: e.tensor_scalar(out=s_stat[:, 4:8],
            in0=s_meanA[:, 4 * s_:4 * s_ + 4], scalar1=-1.0, scalar2=None,
            op0=ALU.mult), deps=[VD])
        t_nmr = OP(V, lambda e, s_=sc: e.tensor_tensor(out=s_stat[:, 8:12],
            in0=s_rstdA[:, 4 * s_:4 * s_ + 4], in1=s_stat[:, 4:8],
            op=ALU.mult), deps=[VD], inc=True)
        TTOFF = [CFIN, CFIN + 2048, CT, 36864]
        t_tt = {}
        for nt in range(4):
            tdeps = [t_nmr, t_rx1]
            if nt < 2:
                tdeps += [t_vmg[(1, 15)]]
            t_tt[nt] = OP(SC_, lambda e, nn=nt, oo=TTOFF[nt]: e.activation(
                out=fv(oo, 2048),
                in_=fv(CX1, 8192)[:, 2048 * nn:2048 * (nn + 1)], func=AF.Identity,
                scale=s_stat[:, 0 + nn:1 + nn], bias=s_stat[:, 8 + nn:9 + nn]),
                deps=tdeps, inc=(nt == 3))
        for e_ in range(4):
            for kh in range(2):
                wi = e_ * 2 + kh
                wodeps = [st2["wo"].get(wi - 2)]
                if sc == 0 and wi == 0:
                    wodeps += [t_wobf, t_vseclast[3]]
                if sc > 0 and wi < 2:
                    wodeps.append(st2["wo"].get(wi + 6))
                t_wol = DMA(SY, bvv(CWOS + (wi % 2) * 2048, 4096).rearrange(
                    "p (k w) -> p k w", w=512),
                    d_wobf[1024 * kh:1024 * (kh + 1),
                           512 * e_:512 * (e_ + 1)].rearrange(
                        "(k p) w -> p k w", p=128), d_wo[wi % 2], deps=wodeps)
                t_mm = None
                for cpar in range(2):
                    for ns in range(2):
                        bank = psb(cpar * 2 + ns)
                        mdeps = []
                        if kh == 0 and cpar == 0 and ns == 0:
                            mdeps = [t_wol, t_vmg[(1, 15)]]
                            if e_ == 0:
                                mdeps.append(t_mm_carry_free := t_c8e)
                            else:
                                mdeps.append(t_opev[(sc, e_ - 1, 3)])
                        elif cpar == 0 and ns == 0:
                            mdeps = [t_wol]
                        for k in range(8):
                            dt_ = 8 * kh + k
                            t_mm = OP(T, lambda e, b=bank, cc_=cpar, nn=ns,
                                      dd=dt_, kk=k, wpar=(wi % 2): e.matmul(
                                b, bvv(CVMG + cc_ * 2048, 4096)[
                                    :, 256 * dd + 128 * nn:256 * dd + 128 * nn + 128],
                                bvv(CWOS + wpar * 2048, 4096)[:, 512 * kk:512 * (kk + 1)],
                                start=(dd == 0), stop=(dd == 15)),
                                deps=mdeps if k == 0 else [],
                                inc=(k == 7 and cpar == 1 and ns == 1))
                st2["wo"][wi] = t_mm
            for cpar in range(2):
                for ns in range(2):
                    nt = cpar * 2 + ns
                    t_opev[(sc, e_, nt)] = OP(V, lambda e, b=psb(nt), nn=nt,
                        ee=e_, oo=TTOFF[nt]: e.tensor_tensor(
                        out=fv(CX1, 8192)[:, 2048 * nn + 512 * ee:
                                          2048 * nn + 512 * ee + 512],
                        in0=b,
                        in1=fv(oo, 2048)[:, 512 * ee:512 * (ee + 1)],
                        op=ALU.add), deps=[t_mm, t_tt[3]], inc=True)
        t_oplast = t_oplastprev = t_opev[(sc, 3, 3)]
        t_x2store[sc] = DMA(SY, d_x2[512 * sc:512 * (sc + 1), :].rearrange(
            "(t p) d -> p t d", p=128),
            fv(CX1, 8192).rearrange("p (t d) -> p t d", d=D), d_x2w,
            deps=[t_oplast])

    t_x2F = Tok(SY, d_x2w, d_x2w.n)
    st["xn1_free"] = t_x2F
    st["w1j"][30] = st["w1j"][31] = st2["wo"][7]
    st["w2j"][30] = st["w2j"][31] = st2["wo"][7]
    c_stores = []
    cg_deps = {gg: [t_x2F] for gg in range(GROUPS)}
    cg_deps[0] += [t_oplast, t_hmlast]
    emit_ffn(d_x2, x_out, d_xc, d_out, cg_deps[0], c_stores,
             g_deps={1: cg_deps[1], 2: cg_deps[2], 3: cg_deps[3]})
    sch.wait(SY, d_out, d_out.n)

    _realize(nc, sch)
    es.close()
    return nc


def _realize(nc, sch):
    with nc.Block() as block:
        for eng_name, starter in [("sync", block.sync), ("tensor", block.tensor),
                                  ("vector", block.vector),
                                  ("scalar", block.scalar),
                                  ("gpsimd", block.gpsimd)]:
            def make(ops):
                def body(eng):
                    for f in ops:
                        f(eng)
                return body
            starter(make(sch.ops[eng_name]))


_BF = ml_dtypes.bfloat16


def _shard_inputs(L0_attn_x, L0_attn_norm_scale, L0_attn_conv_w, L0_attn_conv_b,
                  L0_attn_in_proj_w, L0_attn_dt_bias, L0_attn_head_mix,
                  L0_attn_out_proj_w, L0_ffn_w1, L0_ffn_b1, L0_ffn_w2, L0_ffn_b2):
    f32 = lambda a: np.ascontiguousarray(np.asarray(a), dtype=np.float32)
    X = f32(L0_attn_x)
    W1T = np.ascontiguousarray(f32(L0_ffn_w1).T)
    W2T = np.ascontiguousarray(f32(L0_ffn_w2).T)
    WPT = np.ascontiguousarray(f32(L0_attn_in_proj_w).T)
    WOT = np.ascontiguousarray(f32(L0_attn_out_proj_w).T)
    nsr = f32(L0_attn_norm_scale).reshape(1, D)
    cwc = np.ascontiguousarray(
        f32(L0_attn_conv_w).reshape(16, 128, 4).transpose(1, 0, 2).reshape(128, 64))
    cbc = np.ascontiguousarray(f32(L0_attn_conv_b).reshape(16, 128).T)
    dtbc = np.ascontiguousarray(f32(L0_attn_dt_bias).reshape(16, 128).T)
    b1c = np.ascontiguousarray(f32(L0_ffn_b1).reshape(64, 128).T)
    b2r = f32(L0_ffn_b2).reshape(1, D)
    hmr = np.ascontiguousarray(f32(L0_attn_head_mix).reshape(1, 256))
    eye = np.eye(128, dtype=_BF)
    ones1 = np.ones((1, 128), dtype=_BF)
    tri32 = np.triu(np.ones((32, 32), np.float32)).astype(_BF)
    stri32 = np.triu(np.ones((32, 32), np.float32), 1).astype(_BF)
    shared = dict(wpT=WPT, woT=WOT, nsr=nsr, cwc=cwc, cbc=cbc, dtbc=dtbc,
                  b1c=b1c, b2r=b2r, hmr=hmr, eye=eye, ones1=ones1,
                  tri32=tri32, stri32=stri32)
    maps = []
    for c in range(8):
        b, gr = c // 4, c % 4
        m = dict(shared)
        m["x"] = np.ascontiguousarray(X[b, gr * NT:(gr + 1) * NT, :])
        m["w1t_sl"] = np.ascontiguousarray(W1T[256 * c:256 * (c + 1), :])
        m["w2t_sl"] = np.ascontiguousarray(W2T[1024 * c:1024 * (c + 1), :])
        pos = (np.arange(NT, dtype=np.float32) + gr * NT)
        m["posc"] = np.ascontiguousarray(pos.reshape(16, 128).T)
        hsel = np.zeros((24, 3), np.float32)
        if gr > 0:
            for i in range(3):
                hsel[3 * (c - 1) + i, i] = 1.0
        m["hsel"] = hsel
        csel = np.zeros((32, 8), np.float32)
        for j in range(8):
            csel[8 * gr + j, j] = 1.0
        m["csel"] = csel
        maps.append(m)
    return maps


_NC_CACHE = {}


def kernel(**inputs):
    maps = _shard_inputs(**inputs)
    key = _STOP
    if key not in _NC_CACHE:
        _NC_CACHE[key] = build(stop=_STOP)
    nc = _NC_CACHE[key]
    res = run_bass_kernel_spmd(nc, maps, core_ids=list(range(8)))
    shards = [np.asarray(r["out"]) for r in res.results]
    out = np.empty((2, 8192, D), np.float32)
    for c in range(8):
        out[c // 4, (c % 4) * NT:(c % 4 + 1) * NT, :] = shards[c]
    return out


def _PART3(c):
    raise NotImplementedError


# version 13
# speedup vs baseline: 1.0728x; 1.0059x over previous
import numpy as np
import ml_dtypes
from contextlib import ExitStack
import concourse.bass as bass
import concourse.mybir as mybir
from concourse.bass_utils import run_bass_kernel_spmd

F32 = mybir.dt.float32
BF16 = mybir.dt.bfloat16
I32 = mybir.dt.int32
AF = mybir.ActivationFunctionType
ALU = mybir.AluOpType

D = 2048
NT = 2048
NTILE = 16
DTL = 16
FF = 8192
GROUPS = 4
NSC = 4
CH = 256
PI = float(np.pi)
TWO_PI = float(2 * np.pi)
LN1E4 = float(np.log(10000.0))

_STOP = None


class Sem:
    def __init__(self, es, nc, name, engine_sem=False):
        self.h = es.enter_context(nc.semaphore(name))
        self.n = 0
        self.engine_sem = engine_sem

    def plan(self, k=1):
        self.n += k
        return self.n


class Tok:
    __slots__ = ("eng", "sem", "val")

    def __init__(self, eng, sem, val):
        self.eng, self.sem, self.val = eng, sem, val


class Sched:
    ENGS = ("sync", "tensor", "vector", "scalar", "gpsimd")

    def __init__(self, es, nc):
        self.nc = nc
        self.ops = {e: [] for e in self.ENGS}
        self.last_is_drain = {e: False for e in self.ENGS}
        self.wm = {e: {} for e in self.ENGS}
        self.esems = {e: Sem(es, nc, "se_" + e, engine_sem=True) for e in
                      ("tensor", "vector", "scalar", "gpsimd")}

    def _raw(self, eng, fn):
        self.ops[eng].append(fn)
        self.last_is_drain[eng] = False

    def drain(self, eng):
        if not self.last_is_drain[eng]:
            self.ops[eng].append(lambda e: e.drain())
            self.last_is_drain[eng] = True

    def wait(self, eng, sem, val):
        key = id(sem.h)
        if self.wm[eng].get(key, 0) >= val:
            return
        self.wm[eng][key] = val
        self._raw(eng, lambda e, s=sem.h, v=val: e.wait_ge(s, v))

    def dep(self, eng, deps):
        for t in deps:
            if t is None:
                continue
            if t.sem is None:
                assert t.eng == eng, "cross-engine dep without semaphore"
                self.drain(eng)
            elif t.eng == eng and t.sem.engine_sem:
                self.drain(eng)
            else:
                self.wait(eng, t.sem, t.val)

    def op(self, eng, fn, deps=(), inc=False):
        self.dep(eng, deps)
        if inc:
            sem = self.esems[eng]
            v = sem.plan(1)
            self._raw(eng, lambda e, f=fn, s=sem.h: f(e).then_inc(s, 1))
            return Tok(eng, sem, v)
        self._raw(eng, fn)
        return Tok(eng, None, None)

    def dma(self, q, out, in_, sem, deps=()):
        self.dep(q, deps)
        v = sem.plan(16)
        self._raw(q, lambda e, o=out, i=in_, s=sem.h:
                  e.dma_start(out=o, in_=i).then_inc(s, 16))
        return Tok(q, sem, v)

    def cc(self, kind, alu, groups, ins, outs, sem, deps=()):
        self.dep("gpsimd", deps)
        v = sem.plan(1)
        self._raw("gpsimd", lambda e: e.collective_compute(
            kind, alu, replica_groups=groups, ins=ins, outs=outs).then_inc(sem.h, 1))
        return Tok("gpsimd", sem, v)


def build(stop=None):
    nc = bass.Bass("TRN2", target_bir_lowering=False)
    es = ExitStack()

    def xin(name, shape, dt=F32):
        return nc.dram_tensor(name, shape, dt, kind="ExternalInput").ap()

    x_x = xin("x", [NT, D])
    x_w1sl = xin("w1t_sl", [256, FF])
    x_w2sl = xin("w2t_sl", [1024, D])
    x_wpT = xin("wpT", [D, 3 * D])
    x_woT = xin("woT", [D, D])
    x_nsr = xin("nsr", [1, D])
    x_cwc = xin("cwc", [128, 64])
    x_cbc = xin("cbc", [128, 16])
    x_dtbc = xin("dtbc", [128, 16])
    x_b1c = xin("b1c", [128, 64])
    x_b2r = xin("b2r", [1, D])
    x_hmr = xin("hmr", [1, 256])
    x_posc = xin("posc", [128, 16])
    x_eye = xin("eye", [128, 128], BF16)
    x_ones1 = xin("ones1", [1, 128], BF16)
    x_tri = xin("tri32", [32, 32], BF16)
    x_stri = xin("stri32", [32, 32], BF16)
    x_hsel = xin("hsel", [24, 3])
    x_csel = xin("csel", [32, 8])
    x_out = nc.dram_tensor("out", [NT, D], F32, kind="ExternalOutput").ap()

    def dint(name, shape, dt):
        return nc.dram_tensor(name, shape, dt, kind="Internal").ap()

    def dsh(name, shape, dt):
        return nc.dram_tensor(name, shape, dt, kind="Internal",
                              addr_space="Shared").ap()

    d_w1qinA = dint("w1q_inA", [256, FF // 2], BF16)
    d_w1qinB = dint("w1q_inB", [256, FF // 2], BF16)
    d_w1qA = dsh("w1qA", [D, FF // 2], BF16)
    d_w1qB = dsh("w1qB", [D, FF // 2], BF16)
    d_w2qin_e = [dint("w2q_in%d" % e, [1024, 512], BF16) for e in range(4)]
    d_w2q_e = [dsh("w2q%d" % e, [FF, 512], BF16) for e in range(4)]
    d_arin = dint("ar_in", [1, 2], F32)
    d_arout = dint("ar_out", [1, 2], F32)
    d_arin2 = dint("ar_in2", [1, 2], F32)
    d_arout2 = dint("ar_out2", [1, 2], F32)
    d_red = dint("red_d", [128, 2], F32)
    d_red2 = dint("red2_d", [128, 8], F32)
    d_wpbf = dint("wp_bf", [D, 3 * D], BF16)
    d_wobf = dint("wo_bf", [D, D], BF16)
    d_x1 = dint("x1_d", [NT, D], F32)
    d_x2 = dint("x2_d", [NT, D], F32)
    d_co = dint("co_d", [8 * 128, DTL * CH], BF16)
    d_es = dint("es_d", [8 * 128, DTL * CH], BF16)
    d_ga = dint("ga_d", [8 * 128, DTL * CH], BF16)
    d_hin = dint("halo_in", [3, D], BF16)
    d_hout = dint("halo_out", [24, D], BF16)
    d_cin = dint("agc_in", [16, D], F32)
    d_cout = dint("agc_out", [64, D], F32)

    def sbt(name, shape, dt=F32):
        return es.enter_context(nc.sbuf_tensor(name, shape, dt))

    s_eye = sbt("s_eye", [128, 128], BF16)
    s_ones1 = sbt("s_ones1", [1, 128], BF16)
    s_tri = sbt("s_tri", [32, 32], BF16)
    s_stri = sbt("s_stri", [32, 32], BF16)
    s_hsel = sbt("s_hsel", [24, 3], BF16)
    s_csel = sbt("s_csel", [32, 8], BF16)
    s_cwc = sbt("s_cwc", [128, 64])
    s_cbc = sbt("s_cbc", [128, 16])
    s_dtbc = sbt("s_dtbc", [128, 16])
    s_b1c = sbt("s_b1c", [128, 64])
    s_b2h = sbt("s_b2h", [1, D], BF16)
    s_nsb = sbt("s_nsb", [128, D])
    s_hmb = sbt("s_hmb", [128, 256])
    s_diag = sbt("s_diag", [128, 64 * 128], BF16)
    s_cosT = sbt("s_cosT", [128, NTILE * 64])
    s_sinT = sbt("s_sinT", [128, NTILE * 64])
    s_posc = sbt("s_posc", [128, 16])
    s_invf = sbt("s_invf", [128, 64])
    s_iw1 = sbt("s_iw1", [128, 64])
    s_iw2 = sbt("s_iw2", [128, 64])
    s_iw3 = sbt("s_iw3", [128, 64])
    s_iint = sbt("s_iint", [128, 64], I32)
    s_stat = sbt("s_stat", [128, 64])
    s_rstdA = sbt("s_rstdA", [128, 16])
    s_meanA = sbt("s_meanA", [128, 16])
    s_bdA = sbt("s_bdA", [128, 128])
    s_boA = sbt("s_boA", [128, 128])
    s_lmA = sbt("s_lmA", [128, 128])
    s_corr = sbt("s_corr", [128, 128])
    s_cscr = sbt("s_cscr", [128, 128])
    s_bdh = sbt("s_bdh", [128, 256], BF16)
    s_boh = sbt("s_boh", [128, 128], BF16)
    s_scl = sbt("s_scl", [128, 16])
    s_thr = sbt("s_thr", [128, 2])
    s_halo = sbt("s_halo", [128, 48], BF16)
    s_negl = sbt("s_negl", [128, 32])
    s_hmst = sbt("s_hmst", [128, 2048], BF16)
    s_one = sbt("s_one", [128, 256])

    AW = 39600
    AR = sbt("ARENA", [128, AW])
    ARB = AR[:].bitcast(BF16)

    def fv(off, n):
        assert 0 <= off and off + n <= AW, (off, n)
        return AR[:, off:off + n]

    def bvv(offw, n):
        assert 0 <= offw and 2 * offw + n <= 2 * AW, (offw, n)
        return ARB[:, 2 * offw:2 * offw + n]

    P2A, P2B, P2ABS, P2GE, P2OUT = 0, 8192, 16384, 24576, 32768
    NSRO, HMRO, RED1O = 36864, 38912, 36864
    NSHLO, HMRBO = 32768, 34816
    AXG, AH, AXN1, AXN1T = 0, 8192, 24576, 25600
    AW1S, AW2S = 29696, 33792
    ASCR = 8192
    BDT0, BDT1, BLB = 0, 4096, 8192
    BXN2T = 12288
    BSA = 12288
    BXCT = 16512
    BSB = 16512
    BES = 20608
    BVROT = 24704
    BVV4 = 28800
    BCO = 28800
    BVR = 32896
    BCREP = 33920
    BWPS = 34944
    BGA = 39040
    BAGH = 8192
    KC = dict(a=0, b=2048, c=4096, e=6144, h=8192, c8=10240, c8h=12288, s=13312)
    CFIN, CVMG, CCO, CES, CGA = 0, 4096, 8192, 12288, 16384
    CX1, CT, CHM, CWOS = 20480, 28672, 30720, 32768
    assert CWOS + 4096 <= AW

    PSBIG = es.enter_context(nc.psum_tensor("PSBIG", [128, 2048], F32))[:]
    PS2a = es.enter_context(nc.psum_tensor("PS2a", [128, 512], F32))[:]
    PS2b = es.enter_context(nc.psum_tensor("PS2b", [128, 512], F32))[:]
    PSTa = es.enter_context(nc.psum_tensor("PSTa", [128, 1024], BF16))[:]
    PSTb = es.enter_context(nc.psum_tensor("PSTb", [128, 1024], BF16))[:]
    PST = [PSTa, PSTb]

    def psb(k):
        return PSBIG[:, 512 * k:512 * (k + 1)]

    s_hcol_negl = s_negl[:, 16:32]
    sch = Sched(es, nc)

    def S(name):
        return Sem(es, nc, name)

    s_ar, s_agw1, s_agw2 = S("s_ar"), S("s_agw1"), S("s_agw2")
    s_ar2, s_agw1b = S("s_ar2"), S("s_agw1b")
    s_agh_s, s_agc_s = S("s_aghs"), S("s_agcs")
    d_small, d_gc, d_cast = S("d_small"), S("d_gc"), S("d_cast")
    d_q, d_q2 = S("d_q"), S("d_q2")
    d_qb, d_q2b = S("d_qb"), S("d_q2b")
    d_spl, d_cbg = S("d_spl"), S("d_cbg")
    d_xa, d_x1w = S("d_xa"), S("d_x1w")
    d_w1 = [S("d_w1a"), S("d_w1b")]
    d_w2 = [S("d_w2a"), S("d_w2b")]
    d_wp = [S("d_wpa"), S("d_wpb")]
    d_wo = [S("d_woa"), S("d_wob")]
    d_xb, d_xh = S("d_xb"), S("d_xh")
    d_rlA, d_rlB, d_rlC = S("d_rlA"), S("d_rlB"), S("d_rlC")
    d_gasp = [S("d_gaspa"), S("d_gaspb")]
    d_cosp = S("d_cosp")
    d_x2w, d_xc, d_out, d_cbn = S("d_x2w"), S("d_xc"), S("d_out"), S("d_cbn")
    d_r1, d_r2, d_ai, d_ai2 = S("d_r1"), S("d_r2"), S("d_ai"), S("d_ai2")
    d_td, d_td2 = S("d_td"), S("d_td2")

    OP, DMA, CC = sch.op, sch.dma, sch.cc
    V, SC_, T, G, SY = "vector", "scalar", "tensor", "gpsimd", "sync"
    VD = Tok(V, None, None)
    SD = Tok(SC_, None, None)

    nsr_r = AR[0:1, NSRO:NSRO + D]
    hmr_r = AR[0:1, HMRO:HMRO + 256]
    hmrb_r = ARB[0:1, 2 * HMRBO:2 * HMRBO + 256]
    nshl_r = ARB[0:1, 2 * NSHLO:2 * NSHLO + 2 * D]
    red1_r = AR[64:65, RED1O:RED1O + D]
    agh_r = ARB[0:24, 2 * 36992:2 * 36992 + D]


    t_small = None
    for dst, src in [
        (s_cwc[:], x_cwc), (s_cbc[:], x_cbc), (s_dtbc[:], x_dtbc),
        (s_b1c[:], x_b1c), (nsr_r, x_nsr), (hmr_r, x_hmr),
        (s_posc[:], x_posc), (s_eye[:], x_eye), (s_ones1[:], x_ones1),
        (s_tri[:], x_tri), (s_stri[:], x_stri), (red1_r, x_b2r),
    ]:
        t_small = DMA(SY, dst, src, d_small)
    t_b2l = t_small
    DMA(G, s_hsel[:], x_hsel, d_cast)
    DMA(G, s_csel[:], x_csel, d_cast)
    t_iota = OP(G, lambda e: e.iota(s_iint[:], pattern=[[1, 64]], base=0,
                                    channel_multiplier=0), inc=True)
    t_one = OP(V, lambda e: e.memset(s_one[:], 1.0), inc=True)
    t_thr0 = OP(V, lambda e: e.memset(s_thr[:, 0:2], 0.0), deps=[VD], inc=True)

    t_b2h = OP(V, lambda e: e.tensor_scalar(out=s_b2h[:], in0=red1_r,
        scalar1=0.5, scalar2=None, op0=ALU.mult), deps=[t_b2l], inc=True)
    t_nh = OP(V, lambda e: e.tensor_copy(nshl_r[0:1, 0:D], nsr_r), deps=[t_small])
    t_nl = OP(V, lambda e: e.tensor_tensor(out=nshl_r[0:1, D:2 * D], in0=nsr_r,
        in1=nshl_r[0:1, 0:D], op=ALU.subtract), deps=[VD], inc=True)
    t_nsmm = None
    for q in range(4):
        OP(T, lambda e, qq=q: e.matmul(psb(qq), s_ones1[:],
            nshl_r[0:1, 512 * qq:512 * (qq + 1)], start=True, stop=False),
           deps=[t_nl, t_small])
        t_nsmm = OP(T, lambda e, qq=q: e.matmul(psb(qq), s_ones1[:],
            nshl_r[0:1, D + 512 * qq:D + 512 * (qq + 1)], start=False,
            stop=True), inc=(q == 3))
    t_nsb = OP(V, lambda e: e.tensor_copy(s_nsb[:], PSBIG[:]), deps=[t_nsmm],
               inc=True)
    t_hmc = OP(V, lambda e: e.tensor_copy(hmrb_r, hmr_r), deps=[t_small], inc=True)
    t_hmm = OP(T, lambda e: e.matmul(PS2a[:, 0:256], s_ones1[:], hmrb_r,
        start=True, stop=True), deps=[t_hmc], inc=True)
    t_hmb = OP(V, lambda e: e.tensor_copy(s_hmb[:], PS2a[:, 0:256]),
               deps=[t_hmm], inc=True)
    qA, qB = fv(P2A, 8192), fv(P2B, 8192)
    qabs, qge = fv(P2ABS, 8192), fv(P2GE, 8192)
    qout = bvv(P2OUT, 8192)
    _diagf = s_diag[:].bitcast(F32)
    w2A, w2B = _diagf[:, 0:2048], _diagf[:, 2048:4096]
    w2ge = s_hmst[:, 0:2048]
    w2g2 = s_cosT[:].bitcast(BF16)[:, 0:2048]
    w2a = s_sinT[:].bitcast(BF16)[:, 0:2048]
    w2out = bvv(37888, 2048)

    t_w1l = [None, None]
    for h in range(2):
        t_w1l[h] = DMA(SY, (qA if h == 0 else qB), x_w1sl[128 * h:128 * (h + 1), :],
                       d_q if h == 0 else d_qb)
    t_w2abs = [None] * 8
    t_sgnw2 = [None] * 8
    t_splw2 = [None] * 8
    t_tqw2 = [None] * 8
    for h in range(8):
        buf = w2A if h % 2 == 0 else w2B
        tl = DMA(G, buf, x_w2sl[128 * h:128 * (h + 1), :],
                 d_q2 if h % 2 == 0 else d_q2b,
                 deps=[t_w2abs[h - 2]] if h >= 2 else [])
        t_w2abs[h] = OP(SC_, lambda e, b=buf, hh=h: e.activation(
            out=w2ge, in_=b, func=AF.Abs, accum_out=s_scl[:, 2 + hh:3 + hh]),
            deps=[tl, SD], inc=True)
    t_w1abs = [None, None]
    for h in range(2):
        t_w1abs[h] = OP(SC_, lambda e, b=(qA if h == 0 else qB), hh=h: e.activation(
            out=qabs[:, 0:8192], in_=b, func=AF.Abs,
            accum_out=s_scl[:, hh:hh + 1]),
            deps=[t_w1l[h], SD], inc=True)
    t_rd0 = DMA(G, d_red, s_scl[:, 0:2], d_r1, deps=[t_w1abs[1]])
    t_rd1 = DMA(G, red1_r[0:1, 0:256], d_red.rearrange(
        "p c -> (p c)").unsqueeze(0), d_r1, deps=[t_rd0, t_b2h])
    red3 = red1_r[0:1, 0:256].rearrange("o (p c) -> o p c", c=2)
    t_s1 = OP(V, lambda e: e.reduce_sum(s_thr[0:1, 0:1], red3[:, :, 0:2],
        axis=mybir.AxisListType.XY), deps=[t_rd1, t_thr0])
    t_s3 = OP(V, lambda e: e.tensor_scalar(out=s_thr[0:1, 0:1],
        in0=s_thr[0:1, 0:1], scalar1=0.5 / 16777216.0, scalar2=None,
        op0=ALU.mult), deps=[VD], inc=True)
    t_ari = DMA(G, d_arin, s_thr[0:1, 0:2], d_ai, deps=[t_s3])
    t_ar = CC("AllReduce", ALU.add, [list(range(8))], [d_arin.opt()],
              [d_arout.opt()], s_ar, deps=[t_ari])
    t_rd0b = DMA(G, d_red2, s_scl[:, 2:10], d_r2, deps=[t_w2abs[-1]])
    t_rd1b = DMA(G, red1_r[0:1, 256:1280], d_red2.rearrange(
        "p c -> (p c)").unsqueeze(0), d_r2, deps=[t_rd0b])
    red3b = red1_r[0:1, 256:1280].rearrange("o (p c) -> o p c", c=8)
    t_s2 = OP(V, lambda e: e.reduce_sum(s_thr[0:1, 1:2], red3b,
        axis=mybir.AxisListType.XY), deps=[t_rd1b, t_s3])
    t_s4 = OP(V, lambda e: e.tensor_scalar(out=s_thr[0:1, 1:2],
        in0=s_thr[0:1, 1:2], scalar1=0.5 / 16777216.0, scalar2=None,
        op0=ALU.mult), deps=[VD], inc=True)
    t_ari2 = DMA(G, d_arin2, s_thr[0:1, 0:2], d_ai2, deps=[t_s4])
    t_ar2 = CC("AllReduce", ALU.add, [list(range(8))], [d_arin2.opt()],
               [d_arout2.opt()], s_ar2, deps=[t_ari2])
    t_td = None
    for qd in range(4):
        t_td = DMA(G, s_thr[32 * qd:32 * qd + 1, 0:1], d_arout[0:1, 0:1], d_td,
                   deps=[t_ar, t_thr0] if qd == 0 else [])
    t_td2 = None
    for qd in range(4):
        t_td2 = DMA(G, s_thr[32 * qd:32 * qd + 1, 1:2], d_arout2[0:1, 1:2],
                    d_td2, deps=[t_ar2] if qd == 0 else [])
    t_thr = OP(V, lambda e: e.stream_shuffle(s_thr[:, 0:1], s_thr[:, 0:1],
        mask=[0] * 32), deps=[t_td], inc=True)
    t_thr2 = OP(V, lambda e: e.stream_shuffle(s_thr[:, 1:2], s_thr[:, 1:2],
        mask=[0] * 32), deps=[t_td2, VD], inc=True)
    t_spl1 = []
    t_tq1 = [None, None]
    for h in range(2):
        buf = qA if h == 0 else qB
        OP(V, lambda e, b=buf: e.tensor_scalar(
            out=qge[:, 0:8192], in0=b, scalar1=s_thr[:, 0:1], scalar2=None,
            op0=ALU.is_ge), deps=[t_thr, VD], inc=True)
        OP(V, lambda e, b=buf: e.tensor_scalar(
            out=qabs[:, 0:8192], in0=b, scalar1=-1.0, scalar2=s_thr[:, 0:1],
            op0=ALU.mult, op1=ALU.is_ge), deps=[VD])
        OP(V, lambda e: e.tensor_tensor(out=qge[:, 0:8192],
            in0=qge[:, 0:8192], in1=qabs[:, 0:8192], op=ALU.add), deps=[VD])
        OP(V, lambda e, b=buf: e.tensor_scalar(
            out=qabs[:, 0:8192], in0=b, scalar1=1e38, scalar2=1.0,
            op0=ALU.mult, op1=ALU.min), deps=[VD])
        t_tq1[h] = OP(V, lambda e: e.scalar_tensor_tensor(out=qout[:, 0:8192],
            in0=qabs[:, 0:8192], scalar=-1.0, in1=qge[:, 0:8192],
            op0=ALU.max, op1=ALU.mult),
            deps=[VD, t_spl1[1] if h else None], inc=True)
        t_spl1.append(DMA(G, d_w1qinA[128 * h:128 * (h + 1), :],
                          qout[:, 0:4096], d_spl, deps=[t_tq1[h]]))
        t_spl1.append(DMA(G, d_w1qinB[128 * h:128 * (h + 1), :],
                          qout[:, 4096:8192], d_spl, deps=[]))
    t_agw1 = CC("AllGather", ALU.bypass, [list(range(8))],
                [d_w1qinA.opt()], [d_w1qA.opt()], s_agw1,
                deps=[t_spl1[3]])
    t_agw1b = CC("AllGather", ALU.bypass, [list(range(8))],
                 [d_w1qinB.opt()], [d_w1qB.opt()], s_agw1b,
                 deps=[t_spl1[3]])
    for h in range(8):
        buf = w2A if h % 2 == 0 else w2B
        tl2 = DMA(SY, buf, x_w2sl[128 * h:128 * (h + 1), :],
                  d_q2 if h % 2 == 0 else d_q2b,
                  deps=[t_sgnw2[h - 2]] if h >= 2 else [t_w2abs[6 + h]])
        OP(V, lambda e, b=buf: e.tensor_scalar(
            out=w2ge, in0=b, scalar1=s_thr[:, 1:2], scalar2=None,
            op0=ALU.is_ge), deps=[tl2, t_thr2, VD, t_w2abs[7] if h == 0 else None],
            inc=True)
        OP(V, lambda e, b=buf: e.tensor_scalar(
            out=w2g2, in0=b, scalar1=-1.0, scalar2=s_thr[:, 1:2],
            op0=ALU.mult, op1=ALU.is_ge), deps=[VD])
        OP(V, lambda e: e.tensor_tensor(out=w2ge,
            in0=w2ge, in1=w2g2, op=ALU.add), deps=[VD])
        t_sgnw2[h] = OP(V, lambda e, b=buf: e.tensor_scalar(
            out=w2a, in0=b, scalar1=1e38, scalar2=0.5,
            op0=ALU.mult, op1=ALU.min), deps=[VD], inc=True)
        t_tqw2[h] = OP(V, lambda e: e.scalar_tensor_tensor(out=w2out,
            in0=w2a, scalar=-0.5, in1=w2ge, op0=ALU.max, op1=ALU.mult),
            deps=[VD, t_splw2[h - 1] if h else None], inc=True)
        for e4 in range(4):
            t_splw2[h] = DMA(G, d_w2qin_e[e4][128 * h:128 * (h + 1), :],
                             w2out[:, 512 * e4:512 * (e4 + 1)],
                             d_spl, deps=[t_tqw2[h]] if e4 == 0 else [])
    t_agw2_e = []
    for e4 in range(4):
        t_agw2_e.append(CC("AllGather", ALU.bypass, [list(range(8))],
                           [d_w2qin_e[e4].opt()], [d_w2q_e[e4].opt()], s_agw2,
                           deps=[t_splw2[-1]] if e4 == 0 else []))
    t_agw2 = t_agw2_e[-1]
    t_sgn = {8: t_tq1[1], 9: t_tqw2[7]}
    t_spl = {1: t_spl1[3], 9: t_splw2[7], "q1": t_tq1[1]}
    DMA(G, d_wpbf, x_wpT, d_cast)
    DMA(G, d_wobf, x_woT, d_cast)
    t_castF = Tok(G, d_cast, d_cast.n)
    t_hsel = t_csel = t_wpbf = t_wobf = t_castF

    t_if0 = OP(V, lambda e: e.tensor_copy(s_invf[:], s_iint[:]), deps=[t_iota])
    t_if1 = OP(V, lambda e: e.tensor_scalar(out=s_invf[:], in0=s_invf[:],
        scalar1=-LN1E4 / 64.0, scalar2=None, op0=ALU.mult), deps=[VD], inc=True)
    t_invf = OP(SC_, lambda e: e.activation(out=s_invf[:], in_=s_invf[:],
        func=AF.Exp), deps=[t_if1, t_small], inc=True)
    t_sin = [None] * NTILE
    t_cos = [None] * NTILE
    prev_g = None
    for t in range(NTILE):
        tf = OP(SC_, lambda e, tt=t: e.activation(out=s_iw1[:], in_=s_invf[:],
            func=AF.Identity, scale=s_posc[:, tt:tt + 1]),
            deps=[t_invf, prev_g], inc=True)
        tk = OP(V, lambda e: e.tensor_scalar(out=s_iint[:], in0=s_iw1[:],
            scalar1=1.0 / TWO_PI, scalar2=None, op0=ALU.mult),
            deps=[tf, t_sin[t - 1] if t else None])
        tk2 = OP(V, lambda e: e.tensor_copy(s_iw2[:], s_iint[:]), deps=[VD])
        OP(V, lambda e: e.scalar_tensor_tensor(out=s_iw2[:], in0=s_iw2[:],
            scalar=-TWO_PI, in1=s_iw1[:], op0=ALU.mult, op1=ALU.add),
           deps=[VD])
        OP(V, lambda e: e.tensor_scalar(out=s_iw1[:], in0=s_iw2[:],
            scalar1=PI, scalar2=None, op0=ALU.is_gt), deps=[VD])
        OP(V, lambda e: e.scalar_tensor_tensor(out=s_iw2[:], in0=s_iw1[:],
            scalar=-TWO_PI, in1=s_iw2[:], op0=ALU.mult, op1=ALU.add), deps=[VD])
        OP(V, lambda e: e.tensor_scalar(out=s_iw1[:], in0=s_iw2[:],
            scalar1=-PI, scalar2=None, op0=ALU.is_lt), deps=[VD])
        tr = OP(V, lambda e: e.scalar_tensor_tensor(out=s_iw2[:], in0=s_iw1[:],
            scalar=TWO_PI, in1=s_iw2[:], op0=ALU.mult, op1=ALU.add),
            deps=[VD], inc=True)
        t_sin[t] = OP(SC_, lambda e, tt=t: e.activation(
            out=s_sinT[:, 64 * tt:64 * (tt + 1)], in_=s_iw2[:], func=AF.Sin),
            deps=[tr] + ([t_tqw2[7]] if t == 0 else []), inc=True)
        trc = OP(V, lambda e: e.tensor_scalar(out=s_iw3[:], in0=s_iw2[:],
            scalar1=PI / 2, scalar2=None, op0=ALU.add),
            deps=[tr, t_cos[t - 1] if t else None])
        tg = OP(V, lambda e: e.tensor_scalar(out=s_iw1[:], in0=s_iw3[:],
            scalar1=PI, scalar2=None, op0=ALU.is_gt), deps=[VD])
        prev_g = OP(V, lambda e: e.scalar_tensor_tensor(out=s_iw3[:],
            in0=s_iw1[:], scalar=-TWO_PI, in1=s_iw3[:], op0=ALU.mult,
            op1=ALU.add), deps=[VD], inc=True)
        t_cos[t] = OP(SC_, lambda e, tt=t: e.activation(
            out=s_cosT[:, 64 * tt:64 * (tt + 1)], in_=s_iw3[:], func=AF.Sin),
            deps=[prev_g] + ([t_tqw2[7]] if t == 0 else []), inc=True)

    t_diag = None
    for j4 in range(4):
        t_diag = OP(V, lambda e, jj=j4: e.tensor_tensor(
            out=s_diag[:, 2048 * jj:2048 * (jj + 1)].rearrange(
                "p (i w) -> p i w", w=128),
            in0=s_eye[:].unsqueeze(1).broadcast_to([128, 16, 128]),
            in1=s_cwc[:, 16 * jj:16 * (jj + 1)].unsqueeze(2).broadcast_to(
                [128, 16, 128]),
            op=ALU.mult),
            deps=[t_small] + ([t_tqw2[7], t_splw2[7]] if j4 == 0 else [VD]),
            inc=(j4 == 3))

    hfat = bvv(AH, 32768)
    xn1b = bvv(AXN1, 2048)
    xn1T = bvv(AXN1T, 8192)

    def w1slab(par):
        return bvv(AW1S + par * 2048, 4096)

    def w2slab(par):
        return bvv(AW2S + par * 2048, 4096)

    st = {"w1j": {}, "w2j": {}, "first": True, "pst_free": [None, None],
          "xn1_free": None, "w1l": None}

    def emit_ffn(src, dst, d_load, d_store, xl0_deps, store_tokens, g_deps=None):
        xg = fv(AXG, 8192)
        xg4 = xg.rearrange("p (t d) -> p t d", d=D)
        t_store = None
        t_gelu = {}
        t_ev2 = {}
        first = st["first"]
        for g in range(GROUPS):
            srcv = src[g * 512:(g + 1) * 512, :].rearrange("(t p) d -> p t d", p=128)
            dstv = dst[g * 512:(g + 1) * 512, :].rearrange("(t p) d -> p t d", p=128)
            xldeps = (list(xl0_deps) if g == 0 else [t_store])
            if g_deps:
                xldeps += g_deps.get(g, [])
            t_xl = DMA(SY, xg4, srcv, d_load, deps=xldeps)
            t_ss = t_sm = None
            for t4 in range(4):
                xt = xg[:, 2048 * t4:2048 * (t4 + 1)]
                t_ss = OP(SC_, lambda e, x_=xt, tt=t4: e.activation(
                    out=fv(ASCR, 2048), in_=x_, func=AF.Square,
                    accum_out=s_stat[:, 4 + tt:5 + tt]),
                    deps=[t_xl] + ([t_sgn[8]] if first and g == 0 and t4 == 0 else [SD]),
                    inc=(t4 == 3))
                t_sm = OP(V, lambda e, x_=xt, tt=t4: e.tensor_scalar(
                    out=fv(ASCR + 2048, 2048), in0=x_, scalar1=1.0, scalar2=0.0,
                    op0=ALU.mult, op1=ALU.add,
                    accum_out=s_stat[:, tt:tt + 1]), deps=[t_xl, VD],
                    inc=(t4 == 3))
            OP(V, lambda e: e.tensor_scalar(out=s_stat[:, 8:16],
                in0=s_stat[:, 0:8], scalar1=1.0 / D, scalar2=None,
                op0=ALU.mult), deps=[t_ss, t_sm])
            OP(V, lambda e: e.tensor_tensor(out=s_stat[:, 16:20],
                in0=s_stat[:, 8:12], in1=s_stat[:, 8:12], op=ALU.mult), deps=[VD])
            OP(V, lambda e: e.tensor_tensor(out=s_stat[:, 20:24],
                in0=s_stat[:, 12:16], in1=s_stat[:, 16:20], op=ALU.subtract),
               deps=[VD])
            t_ve = OP(V, lambda e: e.tensor_scalar(out=s_stat[:, 24:28],
                in0=s_stat[:, 20:24], scalar1=1e-5, scalar2=None, op0=ALU.add),
                deps=[VD], inc=True)
            t_lnv = OP(SC_, lambda e: e.activation(out=s_stat[:, 28:32],
                in_=s_stat[:, 24:28], func=AF.Ln), deps=[t_ve], inc=True)
            t_half = OP(V, lambda e: e.tensor_scalar(out=s_stat[:, 32:36],
                in0=s_stat[:, 28:32], scalar1=-0.5, scalar2=None,
                op0=ALU.mult), deps=[t_lnv], inc=True)
            t_rstd = OP(SC_, lambda e: e.activation(out=s_stat[:, 36:40],
                in_=s_stat[:, 32:36], func=AF.Exp), deps=[t_half], inc=True)
            OP(V, lambda e: e.tensor_scalar(out=s_stat[:, 40:44],
                in0=s_stat[:, 8:12], scalar1=-1.0, scalar2=None, op0=ALU.mult),
               deps=[t_rstd])
            t_cb = OP(V, lambda e: e.tensor_tensor(out=s_stat[:, 44:48],
                in0=s_stat[:, 36:40], in1=s_stat[:, 40:44], op=ALU.mult),
                deps=[VD], inc=True)
            t_evx = None
            for t4 in range(4):
                xt = xg[:, 2048 * t4:2048 * (t4 + 1)]
                t_xn1 = OP(SC_, lambda e, x_=xt, tt=t4: e.activation(
                    out=xn1b[:, :], in_=x_, func=AF.Identity,
                    scale=s_stat[:, 36 + tt:37 + tt],
                    bias=s_stat[:, 44 + tt:45 + tt]),
                    deps=[t_cb, st["xn1_free"]] +
                         ([t_spl["q1"]] if first and g == 0 and t4 == 0 else []),
                    inc=True)
                for half in range(2):
                    t_tr = None
                    for j in range(8):
                        dt_ = half * 8 + j
                        t_tr = OP(T, lambda e, hh=half, jj=j, dd=dt_: e.transpose(
                            PST[hh][:, 128 * jj:128 * (jj + 1)],
                            xn1b[:, 128 * dd:128 * (dd + 1)], s_eye[:]),
                            deps=[t_xn1, st["pst_free"][half]] if j == 0 else [],
                            inc=(j == 7))
                    ev_out = xn1T.rearrange("p (k w) -> p k w", w=512)[
                        :, 8 * half:8 * (half + 1), 128 * t4:128 * (t4 + 1)]
                    st["pst_free"][half] = t_evx = OP(
                        V, lambda e, hh=half, o=ev_out: e.tensor_copy(
                            o, PST[hh][:].rearrange("p (k w) -> p k w", w=128)),
                        deps=[t_tr], inc=True)
                st["xn1_free"] = t_evx
            t_xn1T = t_evx
            for f in range(64):
                j, j2 = f // 2, f % 2
                if j2 == 0:
                    par = j % 2
                    wdeps = [st["w1j"].get(j - 2, st["w1j"].get(j + 30))]
                    if first and j < 16:
                        if j < 2:
                            sch.wait(SY, s_agw1, t_agw1.val)
                            wdeps.append(t_spl[1])
                    elif first and j < 18:
                        sch.wait(SY, s_agw1b, t_agw1b.val)
                    w1qsrc = d_w1qA if j < 16 else d_w1qB
                    jj_ = j if j < 16 else j - 16
                    srcw = w1qsrc[:, 256 * jj_:256 * (jj_ + 1)].rearrange(
                        "(k p) w -> p k w", p=128)
                    st["w1l"] = DMA(SY, w1slab(par).rearrange(
                        "p (k w) -> p k w", w=256), srcw, d_w1[par], deps=wdeps)
                psum = PS2a if f % 2 == 0 else PS2b
                mmdeps = [st["w1l"], t_xn1T, t_gelu.get((g, f - 2))]
                if first and g == 0 and f == 0:
                    mmdeps.append(t_hmb)
                t_mm = None
                for k in range(16):
                    lhsT = w1slab(j % 2)[:, k * 256 + j2 * 128:
                                         k * 256 + j2 * 128 + 128]
                    rhs = xn1T[:, 512 * k:512 * (k + 1)]
                    t_mm = OP(T, lambda e, ps=psum, l=lhsT, r=rhs, kk=k: e.matmul(
                        ps, l, r, start=(kk == 0), stop=(kk == 15)),
                        deps=mmdeps if k == 0 else [], inc=(k == 15))
                st["w1j"][j] = t_mm
                gdeps = [t_mm]
                if first and g == 0 and f == 0:
                    gdeps += [t_sgn[8]]
                if first and g == 0 and f == 32:
                    gdeps += [t_sgn[9], t_spl[9]]
                t_gelu[(g, f)] = OP(SC_, lambda e, ps=psum, ff=f: e.activation(
                    out=hfat[:, 512 * ff:512 * (ff + 1)], in_=ps, func=AF.Gelu,
                    bias=s_b1c[:, ff:ff + 1]), deps=gdeps, inc=True)
            for e_ in range(4):
                for h2 in range(8):
                    sidx = e_ * 8 + h2
                    par = sidx % 2
                    wdeps = [st["w2j"].get(sidx - 2, st["w2j"].get(sidx + 30))]
                    if first and g == 0 and h2 == 0:
                        sch.wait(SY, s_agw2, t_agw2_e[e_].val)
                        if e_ == 0:
                            wdeps.append(t_spl[9])
                    srcw = d_w2q_e[e_][1024 * h2:1024 * (h2 + 1), :].rearrange(
                        "(k p) w -> p k w", p=128)
                    t_w2l = DMA(SY, w2slab(par).rearrange(
                        "p (k w) -> p k w", w=512), srcw, d_w2[par], deps=wdeps)
                    t_mm = None
                    for ns in range(4):
                        bank = psb(ns)
                        mmdeps = [t_w2l, t_gelu[(g, 63)]]
                        if h2 == 0:
                            mmdeps.append(
                                t_ev2.get((g, e_ - 1, ns)) if e_ > 0
                                else t_ev2.get((g - 1, 3, ns)))
                            if first and g == 0 and e_ == 0 and ns == 0:
                                mmdeps.append(t_nsb)
                        for k in range(8):
                            ft = 8 * h2 + k
                            lhsT = hfat[:, 512 * ft + 128 * ns:
                                        512 * ft + 128 * ns + 128]
                            rhs = w2slab(par)[:, 512 * k:512 * (k + 1)]
                            t_mm = OP(T, lambda e, b=bank, l=lhsT, r=rhs,
                                      stt=(h2 == 0 and k == 0): e.matmul(
                                b, l, r, start=stt, stop=False),
                                deps=mmdeps if k == 0 else [],
                                inc=(ns == 3 and k == 7))
                    st["w2j"][sidx] = t_mm
                t_mmb = None
                for ns in range(4):
                    t_mmb = OP(T, lambda e, b=psb(ns), ee=e_: e.matmul(
                        b, s_ones1[:], s_b2h[0:1, 512 * ee:512 * (ee + 1)],
                        start=False, stop=True),
                        deps=[t_b2h] if (first and g == 0 and e_ == 0 and
                                         ns == 0) else [], inc=(ns == 3))
                for ns in range(4):
                    xsl = xg[:, 2048 * ns + 512 * e_:2048 * ns + 512 * e_ + 512]
                    t_ev2[(g, e_, ns)] = OP(V, lambda e, b=psb(ns), xx=xsl:
                        e.tensor_tensor(out=xx, in0=b, in1=xx, op=ALU.add),
                        deps=[t_mmb], inc=True)
            t_store = DMA(SY, dstv, xg4, d_store, deps=[t_ev2[(g, 3, 3)]])
            store_tokens.append(t_store)
        st["first"] = False
        return t_store

    a_stores = []
    t_a_last = emit_ffn(x_x, d_x1, d_xa, d_x1w, [t_sgn[8]], a_stores)

    if stop == "A":
        t_dbg = DMA(SY, x_out, d_x1, d_out, deps=[t_a_last])
        sch.wait(SY, d_out, t_dbg.val)
        _realize(nc, sch)
        es.close()
        return nc

    xlast = fv(28800, 2048)
    t_hl = DMA(SY, xlast, d_x1[1920:2048, :], d_xh, deps=[t_a_last])

    def ln2_cols(t_xdep, xtiles, w, sc_idx, dump_off, extra_act_deps=()):
        t_ss = t_sm = None
        for t4 in range(w):
            xt = xtiles[t4]
            t_ss = OP(SC_, lambda e, x_=xt, tt=t4: e.activation(
                out=fv(dump_off, 2048), in_=x_, func=AF.Square,
                accum_out=s_stat[:, 4 + tt:5 + tt]),
                deps=[t_xdep] + list(extra_act_deps) if t4 == 0 else [SD],
                inc=(t4 == w - 1))
            t_sm = OP(V, lambda e, x_=xt, tt=t4: e.tensor_scalar(
                out=fv(dump_off + 2048, 2048), in0=x_, scalar1=1.0, scalar2=0.0,
                op0=ALU.mult, op1=ALU.add, accum_out=s_stat[:, tt:tt + 1]),
                deps=[t_xdep] if t4 == 0 else [VD], inc=(t4 == w - 1))
        OP(V, lambda e: e.tensor_scalar(out=s_stat[:, 8:16], in0=s_stat[:, 0:8],
            scalar1=1.0 / D, scalar2=None, op0=ALU.mult), deps=[t_ss, t_sm])
        OP(V, lambda e: e.tensor_tensor(out=s_stat[:, 16:20], in0=s_stat[:, 8:12],
            in1=s_stat[:, 8:12], op=ALU.mult), deps=[VD])
        OP(V, lambda e: e.tensor_tensor(out=s_stat[:, 20:24], in0=s_stat[:, 12:16],
            in1=s_stat[:, 16:20], op=ALU.subtract), deps=[VD])
        t_ve = OP(V, lambda e: e.tensor_scalar(out=s_stat[:, 24:28],
            in0=s_stat[:, 20:24], scalar1=1e-5, scalar2=None, op0=ALU.add),
            deps=[VD], inc=True)
        t_lnv = OP(SC_, lambda e: e.activation(out=s_stat[:, 28:32],
            in_=s_stat[:, 24:28], func=AF.Ln), deps=[t_ve], inc=True)
        OP(V, lambda e: e.reciprocal(s_stat[:, 32:36], s_stat[:, 24:28]),
           deps=[t_lnv])
        OP(V, lambda e: e.tensor_tensor(out=s_stat[:, 36:40], in0=s_stat[:, 20:24],
            in1=s_stat[:, 32:36], op=ALU.mult), deps=[VD])
        t_m2e = OP(V, lambda e: e.tensor_scalar(out=s_stat[:, 40:44],
            in0=s_stat[:, 36:40], scalar1=1e-6, scalar2=None, op0=ALU.add),
            deps=[VD], inc=True)
        t_ln2 = OP(SC_, lambda e: e.activation(out=s_stat[:, 44:48],
            in_=s_stat[:, 40:44], func=AF.Ln), deps=[t_m2e], inc=True)
        OP(V, lambda e: e.tensor_tensor(out=s_stat[:, 48:52], in0=s_stat[:, 28:32],
            in1=s_stat[:, 44:48], op=ALU.add), deps=[t_ln2])
        t_th = OP(V, lambda e: e.tensor_scalar(out=s_stat[:, 52:56],
            in0=s_stat[:, 48:52], scalar1=-0.5, scalar2=None, op0=ALU.mult),
            deps=[VD], inc=True)
        t_cs = OP(SC_, lambda e: e.activation(out=s_stat[:, 56:60],
            in_=s_stat[:, 52:56], func=AF.Exp), deps=[t_th], inc=True)
        t_th2 = OP(V, lambda e: e.tensor_scalar(out=s_stat[:, 60:64],
            in0=s_stat[:, 28:32], scalar1=-0.5, scalar2=None, op0=ALU.mult),
            deps=[t_cs], inc=True)
        if sc_idx is not None:
            rs_out = s_rstdA[:, 4 * sc_idx:4 * sc_idx + w]
        else:
            rs_out = s_stat[:, 60:60 + w]
        t_rstd = OP(SC_, lambda e, o=rs_out: e.activation(out=o,
            in_=s_stat[:, 60:60 + w] if sc_idx is None else s_stat[:, 60:64],
            func=AF.Exp), deps=[t_th2], inc=True)
        OP(V, lambda e: e.tensor_scalar(out=s_stat[:, 16:20], in0=s_stat[:, 8:12],
            scalar1=-1.0, scalar2=None, op0=ALU.mult), deps=[t_rstd])
        t_cb2 = OP(V, lambda e: e.tensor_tensor(out=s_stat[:, 20:24],
            in0=s_stat[:, 56:60], in1=s_stat[:, 16:20], op=ALU.mult),
            deps=[VD], inc=True)
        t_msave = None
        if sc_idx is not None:
            t_msave = OP(V, lambda e: e.tensor_copy(
                s_meanA[:, 4 * sc_idx:4 * sc_idx + w], s_stat[:, 8:8 + w]),
                deps=[VD], inc=True)
        return t_cs, t_cb2, t_rstd, t_msave

    t_cs0, t_cb20, _, _ = ln2_cols(t_hl, [xlast], 1, None, 30848)
    t_bt = OP(SC_, lambda e: e.activation(out=fv(30848, 2048), in_=xlast,
        func=AF.Identity, scale=s_stat[:, 56:57], bias=s_stat[:, 20:21]),
        deps=[t_cs0, t_cb20], inc=True)
    t_bn = OP(V, lambda e: e.tensor_tensor(out=fv(32896, 2048), in0=fv(30848, 2048),
        in1=s_nsb[:], op=ALU.mult), deps=[t_bt, t_nsb], inc=True)
    t_bc = OP(V, lambda e: e.tensor_scalar(out=bvv(34944, 2048), in0=fv(32896, 2048),
        scalar1=60000.0, scalar2=-60000.0, op0=ALU.min, op1=ALU.max),
        deps=[VD], inc=True)
    t_hi = DMA(G, d_hin, bvv(34944, 2048)[125:128, :], d_gc, deps=[t_bc])
    t_agh = CC("AllGather", ALU.bypass, [list(range(8))], [d_hin.opt()],
               [d_hout.opt()], s_agh_s, deps=[t_hi])
    t_ghl = DMA(G, agh_r, d_hout, d_gc, deps=[t_agh])
    t_selmm = None
    for dc in range(16):
        t_selmm = OP(T, lambda e, d_=dc: e.matmul(
            PSBIG[:, 3 * d_:3 * d_ + 3], agh_r[:, 128 * d_:128 * (d_ + 1)],
            s_hsel[:], start=True, stop=True),
            deps=[t_ghl, t_hsel, t_a_last] if dc == 0 else [], inc=(dc == 15))
    t_hev = OP(V, lambda e: e.tensor_copy(s_halo[:], PSBIG[:, 0:48]),
               deps=[t_selmm], inc=True)

    xn2T = bvv(BXN2T, 8240)
    xctb = bvv(BXCT, 8192)
    st2 = {"wp": {}, "conv_ps": {}, "te": {}, "silu": {}, "gspl": {},
           "vev": {}, "hm": {}, "wo": {}}
    t_dtfree = {}
    t_crlast = {}
    t_cospill = {}
    t_esspill = {}
    t_gaspill = {}
    t_telast = {}
    t_convlast = {}
    t_vseclast = {}
    t_vevlast = {}
    t_scanlast = {}
    prev_bxl_deps = [t_a_last]

    def wpslab(par):
        return bvv(BWPS + par * 2048, 4096)

    for sc in range(NSC):
        cg0 = 2 * sc
        xst = fv(0, 8192)
        xst4 = xst.rearrange("p (t d) -> p t d", d=D)
        t_bxl = DMA(SY, xst4,
                    d_x1[512 * sc:512 * (sc + 1), :].rearrange(
                        "(t p) d -> p t d", p=128), d_xb, deps=prev_bxl_deps)
        xtiles = [xst[:, 2048 * t4:2048 * (t4 + 1)] for t4 in range(4)]
        extra = [t_scanlast[sc - 1]] if sc > 0 else []
        t_cs, t_cb2, t_rstd, t_msave = ln2_cols(t_bxl, xtiles, 4, sc, BLB, extra)
        t_trlast = None
        t_evx = None
        for t4 in range(4):
            tdeps = [t_cs, t_cb2]
            if t4 > 0:
                tdeps.append(t_trlast)
            elif sc == 0:
                tdeps.append(t_hev)
            t_t = OP(SC_, lambda e, x_=xtiles[t4], tt=t4: e.activation(
                out=fv(BLB, 2048), in_=x_, func=AF.Identity,
                scale=s_stat[:, 56 + tt:57 + tt],
                bias=s_stat[:, 20 + tt:21 + tt]), deps=tdeps, inc=True)
            t_n = OP(V, lambda e: e.tensor_tensor(out=fv(BLB + 2048, 2048),
                in0=fv(BLB, 2048), in1=s_nsb[:], op=ALU.mult),
                deps=[t_t], inc=True)
            t_c = OP(V, lambda e: e.tensor_scalar(out=bvv(BLB, 2048),
                in0=fv(BLB + 2048, 2048), scalar1=60000.0, scalar2=-60000.0,
                op0=ALU.min, op1=ALU.max), deps=[VD], inc=True)
            for half in range(2):
                t_tr = None
                for j in range(8):
                    dt_ = half * 8 + j
                    t_tr = OP(T, lambda e, hh=half, jj=j, dd=dt_: e.transpose(
                        PST[hh][:, 128 * jj:128 * (jj + 1)],
                        bvv(BLB, 2048)[:, 128 * dd:128 * (dd + 1)], s_eye[:]),
                        deps=[t_c, st["pst_free"][half]] if j == 0 else [],
                        inc=(j == 7))
                ev_out = xn2T.rearrange("p (k w) -> p k w", w=515)[
                    :, 8 * half:8 * (half + 1), 3 + 128 * t4:3 + 128 * (t4 + 1)]
                evdeps = [t_tr]
                if sc == 0 and t4 == 0 and half == 0:
                    evdeps.append(t_hev)
                if sc > 0 and t4 == 0 and half == 0:
                    evdeps.append(t_scanlast[sc - 1])
                st["pst_free"][half] = t_evx = OP(
                    V, lambda e, hh=half, o=ev_out: e.tensor_copy(
                        o, PST[hh][:].rearrange("p (k w) -> p k w", w=128)),
                    deps=evdeps, inc=True)
            t_trlast = t_tr
        t_hw = OP(V, lambda e: e.tensor_copy(
            xn2T.rearrange("p (k w) -> p k w", w=515)[:, :, 0:3],
            s_halo[:].rearrange("p (k w) -> p k w", w=3)),
            deps=[t_evx, t_hev], inc=True)
        t_hsv = OP(V, lambda e: e.tensor_copy(
            s_halo[:].rearrange("p (k w) -> p k w", w=3),
            xn2T.rearrange("p (k w) -> p k w", w=515)[:, :, 512:515]),
            deps=[VD], inc=True)
        t_silu = {}
        t_convmm = {}
        for dt_ in range(DTL):
            psum = PS2a if dt_ % 2 == 0 else PS2b
            t_mm = None
            for k in range(4):
                cdeps = []
                if k == 0:
                    cdeps = [t_hw, t_hsv, t_diag, t_silu.get(dt_ - 2)]
                    if sc == 0 and dt_ < 2:
                        cdeps.append(t_a_last)
                    if sc > 0 and dt_ < 2:
                        cdeps.append(t_telast[sc - 1])
                t_mm = OP(T, lambda e, ps=psum, dd=dt_, kk=k: e.matmul(
                    ps, s_diag[:, 128 * (4 * dd + kk):128 * (4 * dd + kk + 1)],
                    xn2T[:, 515 * dd + kk:515 * dd + kk + 512],
                    start=(kk == 0), stop=(kk == 3)),
                    deps=cdeps, inc=(k == 3))
            t_convmm[dt_] = t_mm
            sdeps = [t_mm]
            if sc == 0 and dt_ == 0:
                sdeps.append(t_hev)
            if sc > 0 and dt_ == 0:
                sdeps.append(t_scanlast[sc - 1])
            t_silu[dt_] = OP(SC_, lambda e, ps=psum, dd=dt_: e.activation(
                out=xctb.rearrange("p (c w) -> p c w", w=4096)[
                    :, :, 256 * dd:256 * (dd + 1)],
                in_=ps.rearrange("p (c w) -> p c w", w=256),
                func=AF.Silu, bias=s_cbc[:, dd:dd + 1]), deps=sdeps, inc=True)
        t_convlast[sc] = t_convmm[15]
        t_silulast = t_silu[15]
        dtv = fv(BDT0, 8192).rearrange("p (c w) -> p c w", w=4096)
        t_te = {}
        t_dtc = {}
        for o in range(DTL):
            si = o // 2
            if o % 2 == 0:
                par = si % 2
                wdeps = [st2["wp"].get(si - 2)]
                if sc == 0 and si < 2:
                    wdeps += [t_wpbf, t_hev, t_hi]
                if sc > 0 and si < 2:
                    wdeps.append(st2["wp"].get(si + 22))
                st2["wpl"] = DMA(SY, wpslab(par).rearrange(
                    "p (k w) -> p k w", w=256),
                    d_wpbf[:, 256 * si:256 * (si + 1)].rearrange(
                        "(k p) w -> p k w", p=128), d_wp[par], deps=wdeps)
            pbase = 1024 * (o % 2)
            psum2 = PSBIG[:, pbase:pbase + 1024].rearrange(
                "p (c w) -> p c w", w=512)[:, :, 0:256]
            t_mm = None
            for k in range(16):
                mdeps = []
                if k == 0:
                    mdeps = [st2["wpl"], t_silulast, t_te.get(o - 2)]
                    if o < 2:
                        mdeps.append(t_vevlast[sc - 1] if sc > 0 else t_hev)
                for cpar in range(2):
                    t_mm = OP(T, lambda e, oo=o, kk=k, cc_=cpar,
                              pp=par, pb=pbase: e.matmul(
                        PSBIG[:, pb + 512 * cc_:pb + 512 * cc_ + 256],
                        wpslab(pp)[:, 256 * kk + 128 * (oo % 2):
                                   256 * kk + 128 * (oo % 2) + 128],
                        xctb[:, 4096 * cc_ + 256 * kk:4096 * cc_ + 256 * kk + 256],
                        start=(kk == 0), stop=(kk == 15)),
                        deps=mdeps if (k == 0 and cpar == 0) else [],
                        inc=(k == 15 and cpar == 1))
            st2["wp"][si] = t_mm
            tedeps = [t_mm, SD]
            if o == 0:
                tedeps.append(t_crlast[sc - 1] if sc > 0 else None)
            t_te[o] = OP(SC_, lambda e, ps=psum2, oo=o: e.activation(
                out=fv(BCREP, 512).rearrange("p (c w) -> p c w", w=256),
                in_=ps, func=AF.Exp,
                bias=s_dtbc[:, oo:oo + 1]), deps=tedeps, inc=True)
            t_tl = OP(SC_, lambda e: e.activation(out=fv(BCREP + 512, 512),
                in_=fv(BCREP, 512), func=AF.Ln, bias=1.0),
                deps=[SD, t_dtc.get(o - 1)], inc=True)
            t_dtc[o] = OP(V, lambda e, oo=o: e.tensor_scalar(
                out=dtv[:, :, 256 * oo:256 * (oo + 1)],
                in0=fv(BCREP + 512, 512).rearrange("p (c w) -> p c w", w=256),
                scalar1=0.001, scalar2=2.0, op0=ALU.max, op1=ALU.min),
                deps=[t_tl] + ([t_t] if o == 0 else []), inc=True)
        t_telast[sc] = t_te[15]
        t_ropelast = None
        part1_tok = {}

        def scan_part1(cpar, first_deps):
            cg = cg0 + cpar
            dtc = fv(BDT0 + cpar * 4096, 4096)
            LB = fv(BLB, 4096)
            SA = fv(BSA, 4096) if cpar == 0 else fv(BXCT, 4096)
            esf = bvv(BES + cpar * 2048, 4096)
            t_sc = None
            for dt_ in range(DTL):
                scdeps = [VD] if dt_ else ([t_one, t_dtc[15]] + first_deps)
                t_sc = OP(V, lambda e, dd=dt_, L_=LB, d_=dtc: e.tensor_tensor_scan(
                    out=L_[:, 256 * dd:256 * (dd + 1)],
                    data0=ones_ap, data1=d_[:, 256 * dd:256 * (dd + 1)],
                    initial=0.0, op0=ALU.mult, op1=ALU.add), deps=scdeps)
            t_L = OP(V, lambda e, L_=LB: e.tensor_scalar(out=L_[:, 0:4096],
                in0=L_[:, 0:4096], scalar1=-1.0, scalar2=-20.0,
                op0=ALU.mult, op1=ALU.max), deps=[VD], inc=True)
            OP(V, lambda e, cc_=cg, L_=LB: e.tensor_copy(
                s_lmA[:, 16 * cc_:16 * (cc_ + 1)], L_[:, 0::256]), deps=[VD])
            OP(V, lambda e, cc_=cg: e.tensor_scalar(
                out=s_negl[:, 0:16] if cc_ % 2 == 0 else s_hcol_negl,
                in0=s_lmA[:, 16 * cc_:16 * (cc_ + 1)], scalar1=-1.0,
                scalar2=None, op0=ALU.mult), deps=[VD])
            t_bd = OP(V, lambda e, cc_=cg, L_=LB: e.tensor_copy(
                s_bdA[:, cc_::8], L_[:, 255::256]), deps=[VD], inc=True)
            part1_tok[("bd", cpar)] = t_bd
            return t_bd

        def scan_part1b(cpar):
            cg = cg0 + cpar
            LB = fv(BLB, 4096)
            SA = fv(BSA, 4096) if cpar == 0 else fv(BXCT, 4096)
            esf = bvv(BES + cpar * 2048, 4096)
            t_bd = part1_tok[("bd", cpar)]
            t_en = None
            for dt_ in range(DTL):
                endeps = [t_bd] if dt_ == 0 else [SD]
                if dt_ == 0:
                    endeps.append(t_vseclast[sc] if cpar else t_convlast[sc])
                t_en = OP(SC_, lambda e, dd=dt_, A_=SA, L_=LB: e.activation(
                    out=A_[:, 256 * dd:256 * (dd + 1)],
                    in_=L_[:, 256 * dd:256 * (dd + 1)], func=AF.Exp, scale=-1.0,
                    bias=L_[:, 256 * dd:256 * dd + 1]), deps=endeps,
                    inc=(dt_ == 15))
            t_es = None
            for dt_ in range(DTL):
                esdeps = [SD]
                if dt_ == 0 and sc > 0:
                    esdeps.append(t_esspill[(sc - 1, 1)])
                nglap = s_negl[:, 0:16] if cpar == 0 else s_hcol_negl
                t_es = OP(SC_, lambda e, dd=dt_, E_=esf, L_=LB, N_=nglap:
                    e.activation(
                    out=E_[:, 256 * dd:256 * (dd + 1)],
                    in_=L_[:, 256 * dd:256 * (dd + 1)], func=AF.Exp,
                    bias=N_[:, dd:dd + 1]), deps=esdeps, inc=(dt_ == 15))
            part1_tok[cpar] = (t_en, t_es)
            return t_es

        def scan_part2(cpar):
            cg = cg0 + cpar
            dtc = fv(BDT0 + cpar * 4096, 4096)
            SA = fv(BSA, 4096) if cpar == 0 else fv(BXCT, 4096)
            SB = fv(BSB, 4096) if cpar == 0 else fv(BLB, 4096)
            esf = bvv(BES + cpar * 2048, 4096)
            cof = bvv(BCO + cpar * 2048, 4096)
            vrf = bvv(BVROT + cpar * 2048, 4096)
            t_en, t_es = part1_tok[cpar]
            t_A = OP(V, lambda e, B_=SB, d_=dtc, A_=SA: e.tensor_tensor(
                out=B_[:, 0:4096],
                in0=d_, in1=A_[:, 0:4096], op=ALU.mult),
                deps=[t_en, t_es], inc=True)
            t_u = OP(V, lambda e, A_=SA, B_=SB, v_=vrf: e.tensor_tensor(
                out=A_[:, 0:4096],
                in0=B_[:, 0:4096], in1=v_, op=ALU.mult), deps=[VD], inc=True)
            t_cum = None
            for dt_ in range(DTL):
                t_cum = OP(V, lambda e, dd=dt_, A_=SA, B_=SB: e.tensor_tensor_scan(
                    out=B_[:, 256 * dd:256 * (dd + 1)],
                    data0=ones_ap, data1=A_[:, 256 * dd:256 * (dd + 1)],
                    initial=0.0, op0=ALU.mult, op1=ALU.add), deps=[VD])
            codeps = [VD, t_es]
            if sc > 0:
                codeps.append(t_esspill[(sc - 1, 1)])
            t_co = OP(V, lambda e, C_=cof, B_=SB, E_=esf: e.tensor_tensor(
                out=C_, in0=B_[:, 0:4096],
                in1=E_, op=ALU.mult), deps=codeps, inc=True)
            t_bo = OP(V, lambda e, cc_=cg, C_=cof: e.tensor_copy(
                s_boA[:, cc_::8], C_[:, 255::256]), deps=[VD], inc=True)
            t_cospill[(sc, cpar)] = DMA(G, d_co[128 * cg:128 * (cg + 1), :],
                                        cof, d_cosp,
                                        deps=[t_bo])
            t_esspill[(sc, cpar)] = DMA(G, d_es[128 * cg:128 * (cg + 1), :],
                                        esf, d_cosp, deps=[t_co])
            return t_A, t_co, t_es

        ones_ap = s_one[:]
        scan_part1(0, [t_telast[sc], t_trlast])

        t_gs = {}
        for o in range(DTL):
            si = 8 + o // 2
            if o % 2 == 0:
                par = si % 2
                st2["wpl"] = DMA(SY, wpslab(par).rearrange(
                    "p (k w) -> p k w", w=256),
                    d_wpbf[:, 4096 + 256 * (o // 2):4096 + 256 * (o // 2 + 1)
                           ].rearrange("(k p) w -> p k w", p=128), d_wp[par],
                    deps=[st2["wp"].get(si - 2)])
            pbase = 1024 * (o % 2)
            psum2 = PSBIG[:, pbase:pbase + 1024].rearrange(
                "p (c w) -> p c w", w=512)[:, :, 0:256]
            t_mm = None
            for k in range(16):
                mdeps = []
                if k == 0:
                    mdeps = [st2["wpl"], t_gs.get(o - 2, t_te.get(14 + o))]
                for cpar in range(2):
                    t_mm = OP(T, lambda e, oo=o, kk=k, cc_=cpar,
                              pp=par, pb=pbase: e.matmul(
                        PSBIG[:, pb + 512 * cc_:pb + 512 * cc_ + 256],
                        wpslab(pp)[:, 256 * kk + 128 * (oo % 2):
                                   256 * kk + 128 * (oo % 2) + 128],
                        xctb[:, 4096 * cc_ + 256 * kk:4096 * cc_ + 256 * kk + 256],
                        start=(kk == 0), stop=(kk == 15)),
                        deps=mdeps if (k == 0 and cpar == 0) else [],
                        inc=(k == 15 and cpar == 1))
            st2["wp"][si] = t_mm
            gping = bvv(BGA + (o % 2) * 256, 512)
            t_gs[o] = OP(SC_, lambda e, ps=psum2, gp=gping: e.activation(
                out=gp.rearrange("p (c w) -> p c w", w=256),
                in_=ps, func=AF.Sigmoid),
                deps=[t_mm, st2["gspl"].get(o - 2, st2["gspl"].get(o + 14))],
                inc=True)
            st2["gspl"][o] = DMA(G, d_ga.rearrange("(c p) w -> p c w", p=128)[
                :, cg0:cg0 + 2, 256 * o:256 * (o + 1)],
                gping.rearrange("p (c w) -> p c w", w=256), d_gasp[o % 2],
                deps=[t_gs[o]])
        t_gaspill[sc] = st2["gspl"][15]
        t_gslast = t_gs[15]
        st2["gslast"] = (t_gs[14], t_gs[15])
        t_es0 = scan_part1b(0)

        t_vev = {}
        for dv in range(4):
            for kh in range(2):
                si = 16 + dv * 2 + kh
                par = si % 2
                st2["wpl"] = DMA(SY, wpslab(par).rearrange(
                    "p (k w) -> p k w", w=512),
                    d_wpbf[1024 * kh:1024 * (kh + 1),
                           2048 + 512 * dv:2048 + 512 * (dv + 1)].rearrange(
                        "(k p) w -> p k w", p=128), d_wp[par],
                    deps=[st2["wp"].get(si - 2)])
                t_mm = None
                for nt in range(4):
                    bank = psb(nt)
                    mdeps = []
                    if kh == 0 and nt == 0:
                        mdeps = [st2["wpl"], t_vev.get((dv - 1, 3))]
                        if dv == 0:
                            mdeps += [t_gs[15], t_gs[14]]
                    elif nt == 0:
                        mdeps = [st2["wpl"]]
                    for k in range(8):
                        ft = 8 * kh + k
                        t_mm = OP(T, lambda e, b=bank, nn=nt, f_=ft, kk=k,
                                  pp=par: e.matmul(
                            b, xctb[:, 4096 * (nn // 2) + 256 * f_ +
                                    128 * (nn % 2):4096 * (nn // 2) +
                                    256 * f_ + 128 * (nn % 2) + 128],
                            wpslab(pp)[:, 512 * kk:512 * (kk + 1)],
                            start=(f_ == 0), stop=(f_ == 15)),
                            deps=mdeps if k == 0 else [],
                            inc=(k == 7 and nt == 3))
                st2["wp"][si] = t_mm
            for nt in range(4):
                vdeps = [t_mm]
                if sc > 0 and dv == 0 and nt == 0:
                    vdeps += [t_esspill[(sc - 1, 1)]]
                if sc == 0 and dv == 0 and nt == 0:
                    vdeps += [t_hi]
                t_vev[(dv, nt)] = OP(V, lambda e, b=psb(nt), nn=nt, dd=dv:
                    e.tensor_copy(bvv(BVV4 + nn * 1024, 2048)[
                        :, 512 * dd:512 * (dd + 1)], b), deps=vdeps, inc=True)
        t_vseclast[sc] = st2["wp"][23]
        t_vevlast[sc] = t_vev[(3, 3)]
        t_vrot_ev = {}
        for nt in range(4):
            gnt = sc * 4 + nt
            cpar = nt // 2
            t_cr = OP(V, lambda e, gg=gnt: e.tensor_copy(
                bvv(BCREP, 1024).rearrange("p (h w) -> p h w", w=64),
                s_cosT[:, 64 * gg:64 * (gg + 1)].unsqueeze(1).broadcast_to(
                    [128, 16, 64])),
                deps=[t_cos[15], t_vev[(3, nt)], t_dtc[15]], inc=True)
            t_sr = OP(V, lambda e, gg=gnt: e.tensor_copy(
                bvv(BCREP + 512, 1024).rearrange("p (h w) -> p h w", w=64),
                s_sinT[:, 64 * gg:64 * (gg + 1)].unsqueeze(1).broadcast_to(
                    [128, 16, 64])), deps=[t_sin[15], VD], inc=True)
            vt3 = bvv(BVV4 + nt * 1024, 2048).rearrange("p (h w) -> p h w", w=128)
            vr3 = bvv(BVR, 2048).rearrange("p (h w) -> p h w", w=128)
            cr3 = bvv(BCREP, 1024).rearrange("p (h w) -> p h w", w=64)
            sr3 = bvv(BCREP + 512, 1024).rearrange("p (h w) -> p h w", w=64)
            S1 = s_hmst[:, 0:1024].rearrange("p (h w) -> p h w", w=64)
            S2 = s_hmst[:, 1024:2048].rearrange("p (h w) -> p h w", w=64)
            rope_deps = [t_cr, t_sr]
            if nt > 0:
                rope_deps.append(t_vrot_ev[(nt - 1, 1)])
            OP(V, lambda e, v_=vt3, r_=vr3, c_=cr3: e.tensor_tensor(
                out=r_[:, :, 0:64],
                in0=v_[:, :, 0:64], in1=c_, op=ALU.mult), deps=rope_deps)
            OP(V, lambda e, v_=vt3, r_=vr3, s_=sr3: e.tensor_tensor(
                out=r_[:, :, 64:128],
                in0=v_[:, :, 0:64], in1=s_, op=ALU.mult), deps=[VD])
            OP(V, lambda e, v_=vt3, s_=sr3, o_=S1: e.tensor_tensor(
                out=o_, in0=v_[:, :, 64:128],
                in1=s_, op=ALU.mult), deps=[VD])
            OP(V, lambda e, v_=vt3, c_=cr3, o_=S2: e.tensor_tensor(
                out=o_, in0=v_[:, :, 64:128],
                in1=c_, op=ALU.mult), deps=[VD])
            OP(V, lambda e, r_=vr3, o_=S1: e.tensor_tensor(out=r_[:, :, 0:64],
                in0=r_[:, :, 0:64], in1=o_, op=ALU.subtract), deps=[VD])
            t_rope = t_ropelast = OP(V, lambda e, r_=vr3, o_=S2: e.tensor_tensor(
                out=r_[:, :, 64:128],
                in0=r_[:, :, 64:128], in1=o_, op=ALU.add), deps=[VD], inc=True)
            for half in range(2):
                t_tr = None
                for j in range(8):
                    dt_ = half * 8 + j
                    t_tr = OP(T, lambda e, hh=half, jj=j, dd=dt_: e.transpose(
                        PST[hh][:, 128 * jj:128 * (jj + 1)],
                        bvv(BVR, 2048)[:, 128 * dd:128 * (dd + 1)], s_eye[:]),
                        deps=[t_rope, st["pst_free"][half]] if j == 0 else [],
                        inc=(j == 7))
                ev_out = bvv(BVROT + cpar * 2048, 4096).rearrange(
                    "p (k w) -> p k w", w=256)[:, 8 * half:8 * (half + 1),
                                               128 * (nt % 2):128 * (nt % 2) + 128]
                evdeps = [t_tr]
                if sc > 0 and nt == 0 and half == 0:
                    evdeps.append(t_scanlast[sc - 1])
                st["pst_free"][half] = t_vrot_ev[(nt, half)] = OP(
                    V, lambda e, hh=half, o=ev_out: e.tensor_copy(
                        o, PST[hh][:].rearrange("p (k w) -> p k w", w=128)),
                    deps=evdeps, inc=True)
        t_A0, t_co0, _ = scan_part2(0)
        scan_part1(1, [t_es0])
        scan_part1b(1)
        t_A1, t_co1, t_es1 = scan_part2(1)
        t_dtfree[sc] = t_A1
        t_scanlast[sc] = t_co1
        t_crlast[sc] = t_ropelast
        prev_bxl_deps = [t_dtfree[sc]]
        prev_bxl_deps = [t_dtfree[sc]]

    if stop == "B1":
        t_dbg = DMA(G, x_out.rearrange("(r s) d -> r (s d)", s=2), d_co, d_out,
                    deps=[t_esspill[(3, 1)]])
        sch.wait(G, d_out, t_dbg.val)
        _realize(nc, sch)
        es.close()
        return nc

    c32a = AR[0:32, 0:2048]
    c32b = AR[0:32, 2048:4096]
    c32c = AR[0:32, 4096:6144]
    c32e = AR[0:32, 6144:8192]
    c32h = ARB[0:32, 2 * 8192:2 * 8192 + 4096]
    c8 = AR[0:8, 10240:12288]
    c8hb = ARB[0:8, 2 * 12288:2 * 12288 + 2048]
    t_k0 = OP(V, lambda e: e.tensor_copy(s_bdh[:, 0:128], s_bdA[:]),
              deps=[t_dtfree[3], t_scanlast[3]], inc=True)
    OP(V, lambda e: e.tensor_tensor(out=s_bdh[:, 128:256], in0=s_bdA[:],
        in1=s_bdh[:, 0:128], op=ALU.subtract), deps=[VD])
    t_k1 = OP(V, lambda e: e.tensor_copy(s_boh[:], s_boA[:]), deps=[VD], inc=True)
    t_mm = None
    for dt_ in range(DTL):
        OP(T, lambda e, dd=dt_: e.matmul(PSBIG[0:8, 128 * dd:128 * (dd + 1)],
            s_bdh[:, 8 * dd:8 * (dd + 1)], s_eye[:], start=True, stop=False),
           deps=[t_k1, t_vevlast[3]] if dt_ == 0 else [])
        t_mm = OP(T, lambda e, dd=dt_: e.matmul(
            PSBIG[0:8, 128 * dd:128 * (dd + 1)],
            s_bdh[:, 128 + 8 * dd:128 + 8 * (dd + 1)], s_eye[:],
            start=False, stop=True), inc=(dt_ == 15))
    t_bde = OP(V, lambda e: e.tensor_copy(c8, PSBIG[0:8, :]),
               deps=[t_mm, t_scanlast[3]], inc=True)
    t_bdw = DMA(G, d_cin[0:8, :], c8, d_cbn, deps=[t_bde])
    t_mm = None
    for dt_ in range(DTL):
        t_mm = OP(T, lambda e, dd=dt_: e.matmul(
            PSBIG[0:8, 128 * dd:128 * (dd + 1)],
            s_boh[:, 8 * dd:8 * (dd + 1)], s_eye[:], start=True, stop=True),
            deps=[t_bde] if dt_ == 0 else [], inc=(dt_ == 15))
    t_boe = OP(V, lambda e: e.tensor_copy(c8, PSBIG[0:8, :]),
               deps=[t_mm, t_bdw], inc=True)
    t_bow = DMA(G, d_cin[8:16, :], c8, d_cbn, deps=[t_boe])
    t_agc = CC("AllGather", ALU.bypass, [[0, 1, 2, 3], [4, 5, 6, 7]],
               [d_cin.opt()], [d_cout.opt()], s_agc_s, deps=[t_bdw, t_bow])
    cview = d_cout.rearrange("(r k) d -> r k d", k=16)
    t_bdg = DMA(SY, c32a, cview[:, 0:8, :], d_cbg, deps=[t_agc, t_dtfree[3]])
    t_bog = DMA(SY, c32b, cview[:, 8:16, :], d_cbg, deps=[t_agc])
    t_h0 = OP(V, lambda e: e.tensor_copy(c32h[:, 0:2048], c32a),
              deps=[t_bog, t_boe], inc=True)
    t_h1 = OP(V, lambda e: e.tensor_tensor(out=c32h[:, 2048:4096], in0=c32a,
        in1=c32h[:, 0:2048], op=ALU.subtract), deps=[VD], inc=True)
    t_mm = None
    for q in range(4):
        OP(T, lambda e, qq=q: e.matmul(PSBIG[0:32, 512 * qq:512 * (qq + 1)],
            s_tri[:], c32h[:, 512 * qq:512 * (qq + 1)], start=True, stop=False),
           deps=[t_h1, t_boe] if q == 0 else [])
        t_mm = OP(T, lambda e, qq=q: e.matmul(
            PSBIG[0:32, 512 * qq:512 * (qq + 1)], s_tri[:],
            c32h[:, 2048 + 512 * qq:2048 + 512 * (qq + 1)], start=False,
            stop=True), inc=(q == 3))
    t_cd = OP(V, lambda e: e.tensor_scalar(out=c32a, in0=PSBIG[0:32, :],
        scalar1=-80.0, scalar2=0.0, op0=ALU.max, op1=ALU.min),
        deps=[t_mm], inc=True)
    t_stab = OP(V, lambda e: e.stream_shuffle(c32c, c32a, mask=[0] * 32),
                deps=[VD])
    OP(V, lambda e: e.tensor_tensor(out=c32e, in0=c32a, in1=c32c,
        op=ALU.subtract), deps=[VD])
    t_ncd = OP(V, lambda e: e.tensor_scalar(out=c32e, in0=c32e, scalar1=-20.0,
        scalar2=0.0, op0=ALU.max, op1=ALU.min), deps=[VD], inc=True)
    t_eneg = OP(SC_, lambda e: e.activation(out=c32c, in_=c32e, func=AF.Exp,
        scale=-1.0), deps=[t_ncd], inc=True)
    t_epos = OP(SC_, lambda e: e.activation(out=c32a, in_=c32e, func=AF.Exp),
                deps=[SD], inc=True)
    t_seed = OP(V, lambda e: e.tensor_tensor(out=c32e, in0=c32b, in1=c32c,
        op=ALU.mult), deps=[t_epos, t_bog], inc=True)
    OP(V, lambda e: e.tensor_copy(c32h[:, 0:2048], c32e), deps=[VD])
    t_h2 = OP(V, lambda e: e.tensor_tensor(out=c32h[:, 2048:4096], in0=c32e,
        in1=c32h[:, 0:2048], op=ALU.subtract), deps=[VD], inc=True)
    t_mm = None
    for q in range(4):
        OP(T, lambda e, qq=q: e.matmul(PSBIG[0:32, 512 * qq:512 * (qq + 1)],
            s_stri[:], c32h[:, 512 * qq:512 * (qq + 1)], start=True,
            stop=False), deps=[t_h2, t_cd] if q == 0 else [])
        t_mm = OP(T, lambda e, qq=q: e.matmul(
            PSBIG[0:32, 512 * qq:512 * (qq + 1)], s_stri[:],
            c32h[:, 2048 + 512 * qq:2048 + 512 * (qq + 1)], start=False,
            stop=True), inc=(q == 3))
    t_car = OP(V, lambda e: e.tensor_tensor(out=c32c, in0=PSBIG[0:32, :],
        in1=c32a, op=ALU.mult), deps=[t_mm, t_epos], inc=True)
    OP(V, lambda e: e.tensor_copy(c32h[:, 0:2048], c32c), deps=[VD])
    t_h3 = OP(V, lambda e: e.tensor_tensor(out=c32h[:, 2048:4096], in0=c32c,
        in1=c32h[:, 0:2048], op=ALU.subtract), deps=[VD], inc=True)
    t_mm = None
    for q in range(4):
        OP(T, lambda e, qq=q: e.matmul(PSBIG[0:8, 512 * qq:512 * (qq + 1)],
            s_csel[:], c32h[:, 512 * qq:512 * (qq + 1)], start=True,
            stop=False), deps=[t_h3, t_csel, t_car] if q == 0 else [])
        t_mm = OP(T, lambda e, qq=q: e.matmul(
            PSBIG[0:8, 512 * qq:512 * (qq + 1)], s_csel[:],
            c32h[:, 2048 + 512 * qq:2048 + 512 * (qq + 1)], start=False,
            stop=True), inc=(q == 3))
    t_c8e = OP(V, lambda e: e.tensor_copy(c8, PSBIG[0:8, :]),
               deps=[t_mm, t_car], inc=True)
    t_c8h = OP(V, lambda e: e.tensor_copy(c8hb, c8), deps=[VD], inc=True)
    t_mm = None
    for dt_ in range(DTL):
        t_mm = OP(T, lambda e, dd=dt_: e.matmul(PS2b[:, 8 * dd:8 * (dd + 1)],
            c8hb[:, 128 * dd:128 * (dd + 1)], s_eye[0:8, 0:8], start=True,
            stop=True), deps=[t_c8h, st2["gslast"][1]] if dt_ == 0 else [],
            inc=(dt_ == 15))
    t_corr0 = OP(V, lambda e: e.tensor_copy(
        s_corr[:].rearrange("p (c t) -> p t c", t=16),
        PS2b[:, 0:128].rearrange("p (t c) -> p t c", c=8)), deps=[t_mm],
        inc=True)
    t_elm = OP(SC_, lambda e: e.activation(out=s_cscr[:], in_=s_lmA[:],
        func=AF.Exp), deps=[t_epos, t_scanlast[3]], inc=True)
    t_corrF = OP(V, lambda e: e.tensor_tensor(out=s_corr[:], in0=s_corr[:],
        in1=s_cscr[:], op=ALU.mult), deps=[t_corr0, t_elm], inc=True)

    t_x2store = {}
    t_opev = {}
    t_hmlast = None
    t_oplast = None
    t_oplastprev = None
    t_finprev = None
    for sc in range(NSC):
        cg0 = 2 * sc
        t_spF = Tok(G, d_cosp, d_cosp.n)
        rdeps0 = [t_corrF, t_c8e, t_spF]
        if sc > 0:
            rdeps0 = [t_finprev, t_spF]
        t_rco = DMA(SY, bvv(CCO, 8192).rearrange("p (c w) -> p c w", w=4096),
                    d_co.rearrange("(c p) w -> p c w", p=128)[:, cg0:cg0 + 2, :],
                    d_rlA, deps=rdeps0)
        t_res = DMA(SY, bvv(CES, 8192).rearrange("p (c w) -> p c w", w=4096),
                    d_es.rearrange("(c p) w -> p c w", p=128)[:, cg0:cg0 + 2, :],
                    d_rlA, deps=[t_corrF])
        gadeps = [Tok(G, d_gasp[0], d_gasp[0].n), Tok(G, d_gasp[1], d_gasp[1].n)]
        if sc > 0:
            gadeps.append(t_hmlast)
        t_rga = DMA(SY, bvv(CGA, 8192).rearrange("p (c w) -> p c w", w=4096),
                    d_ga.rearrange("(c p) w -> p c w", p=128)[:, cg0:cg0 + 2, :],
                    d_rlB, deps=gadeps)
        t_rx1 = DMA(SY, fv(CX1, 8192).rearrange("p (t d) -> p t d", d=D),
                    d_x1[512 * sc:512 * (sc + 1), :].rearrange(
                        "(t p) d -> p t d", p=128), d_rlC,
                    deps=([t_scanlast[3], t_esspill[(3, 0)], t_esspill[(3, 1)]]
                          if sc == 0 else [t_x2store[sc - 1]]))
        t_fin = {}
        for cpar in range(2):
            cg = cg0 + cpar
            t_f = None
            for dt_ in range(DTL):
                fdeps = []
                if dt_ == 0:
                    fdeps = [t_res, t_corrF]
                    if sc > 0:
                        fdeps.append(t_oplastprev)
                t_f = OP(V, lambda e, cc_=cpar, dd=dt_, g_=cg:
                    e.scalar_tensor_tensor(
                        out=bvv(CFIN + cc_ * 2048, 4096)[:, 256 * dd:256 * (dd + 1)],
                        in0=bvv(CES + cc_ * 2048, 4096)[:, 256 * dd:256 * (dd + 1)],
                        scalar=s_corr[:, 16 * g_ + dd:16 * g_ + dd + 1],
                        in1=bvv(CCO + cc_ * 2048, 4096)[:, 256 * dd:256 * (dd + 1)],
                        op0=ALU.mult, op1=ALU.add), deps=fdeps if dt_ == 0 else [VD],
                    inc=(dt_ == 15))
            t_fin[cpar] = t_f
        t_finprev = t_fin[1]
        t_vmg = {}
        for m in range(16):
            hmdeps = [st2["hm"].get(m - 2, st2["hm"].get(14) if sc > 0 else None)]
            if sc == 0 and m == 0:
                hmdeps += [t_hmb, t_spF]
            t_hml = OP(V, lambda e, mm_=m: e.tensor_tensor(
                out=bvv(CHM + (mm_ % 2) * 1024, 2048).rearrange(
                    "p (i w) -> p i w", w=128),
                in0=s_eye[:].unsqueeze(1).broadcast_to([128, 16, 128]),
                in1=s_hmb[:, mm_::16].unsqueeze(2).broadcast_to([128, 16, 128]),
                op=ALU.mult), deps=hmdeps, inc=True)
            for cpar in range(2):
                psum = PS2a if cpar == 0 else PS2b
                t_mm = None
                for h in range(16):
                    mdeps = []
                    if h == 0:
                        mdeps = [t_hml, t_fin[cpar], t_vmg.get((cpar, m - 1))]
                        if sc == 0 and m == 0:
                            mdeps += [t_corr0, st2["gslast"][cpar]]
                    t_mm = OP(T, lambda e, ps=psum, hh=h, mm_=m, cc_=cpar:
                        e.matmul(ps[:, 0:256],
                                 bvv(CHM + (mm_ % 2) * 1024, 2048)[
                                     :, 128 * hh:128 * (hh + 1)],
                                 bvv(CFIN + cc_ * 2048, 4096)[
                                     :, 256 * hh:256 * (hh + 1)],
                                 start=(hh == 0), stop=(hh == 15)),
                        deps=mdeps if h == 0 else [], inc=(h == 15))
                t_vmg[(cpar, m)] = OP(
                    V, lambda e, ps=psum, mm_=m, cc_=cpar: e.tensor_tensor(
                        out=bvv(CVMG + cc_ * 2048, 4096)[:, 256 * mm_:256 * (mm_ + 1)],
                        in0=ps[:, 0:256],
                        in1=bvv(CGA + cc_ * 2048, 4096)[:, 256 * mm_:256 * (mm_ + 1)],
                        op=ALU.mult), deps=[t_mm, t_rga], inc=True)
            st2["hm"][m] = t_vmg[(1, m)]
        t_hmlast = t_vmg[(1, 15)]
        OP(V, lambda e, s_=sc: e.tensor_scalar(out=s_stat[:, 0:4],
            in0=s_rstdA[:, 4 * s_:4 * s_ + 4], scalar1=1.0, scalar2=None,
            op0=ALU.add), deps=[t_rx1])
        OP(V, lambda e, s_=sc: e.tensor_scalar(out=s_stat[:, 4:8],
            in0=s_meanA[:, 4 * s_:4 * s_ + 4], scalar1=-1.0, scalar2=None,
            op0=ALU.mult), deps=[VD])
        t_nmr = OP(V, lambda e, s_=sc: e.tensor_tensor(out=s_stat[:, 8:12],
            in0=s_rstdA[:, 4 * s_:4 * s_ + 4], in1=s_stat[:, 4:8],
            op=ALU.mult), deps=[VD], inc=True)
        TTOFF = [CFIN, CFIN + 2048, CT, 36864]
        t_tt = {}
        for nt in range(4):
            tdeps = [t_nmr, t_rx1]
            if nt < 2:
                tdeps += [t_vmg[(1, 15)]]
            t_tt[nt] = OP(SC_, lambda e, nn=nt, oo=TTOFF[nt]: e.activation(
                out=fv(oo, 2048),
                in_=fv(CX1, 8192)[:, 2048 * nn:2048 * (nn + 1)], func=AF.Identity,
                scale=s_stat[:, 0 + nn:1 + nn], bias=s_stat[:, 8 + nn:9 + nn]),
                deps=tdeps, inc=(nt == 3))
        for e_ in range(4):
            for kh in range(2):
                wi = e_ * 2 + kh
                wodeps = [st2["wo"].get(wi - 2)]
                if sc == 0 and wi == 0:
                    wodeps += [t_wobf, t_vseclast[3]]
                if sc > 0 and wi < 2:
                    wodeps.append(st2["wo"].get(wi + 6))
                t_wol = DMA(SY, bvv(CWOS + (wi % 2) * 2048, 4096).rearrange(
                    "p (k w) -> p k w", w=512),
                    d_wobf[1024 * kh:1024 * (kh + 1),
                           512 * e_:512 * (e_ + 1)].rearrange(
                        "(k p) w -> p k w", p=128), d_wo[wi % 2], deps=wodeps)
                t_mm = None
                for cpar in range(2):
                    for ns in range(2):
                        bank = psb(cpar * 2 + ns)
                        mdeps = []
                        if kh == 0 and cpar == 0 and ns == 0:
                            mdeps = [t_wol, t_vmg[(1, 15)]]
                            if e_ == 0:
                                mdeps.append(t_mm_carry_free := t_c8e)
                            else:
                                mdeps.append(t_opev[(sc, e_ - 1, 3)])
                        elif cpar == 0 and ns == 0:
                            mdeps = [t_wol]
                        for k in range(8):
                            dt_ = 8 * kh + k
                            t_mm = OP(T, lambda e, b=bank, cc_=cpar, nn=ns,
                                      dd=dt_, kk=k, wpar=(wi % 2): e.matmul(
                                b, bvv(CVMG + cc_ * 2048, 4096)[
                                    :, 256 * dd + 128 * nn:256 * dd + 128 * nn + 128],
                                bvv(CWOS + wpar * 2048, 4096)[:, 512 * kk:512 * (kk + 1)],
                                start=(dd == 0), stop=(dd == 15)),
                                deps=mdeps if k == 0 else [],
                                inc=(k == 7 and cpar == 1 and ns == 1))
                st2["wo"][wi] = t_mm
            for cpar in range(2):
                for ns in range(2):
                    nt = cpar * 2 + ns
                    t_opev[(sc, e_, nt)] = OP(V, lambda e, b=psb(nt), nn=nt,
                        ee=e_, oo=TTOFF[nt]: e.tensor_tensor(
                        out=fv(CX1, 8192)[:, 2048 * nn + 512 * ee:
                                          2048 * nn + 512 * ee + 512],
                        in0=b,
                        in1=fv(oo, 2048)[:, 512 * ee:512 * (ee + 1)],
                        op=ALU.add), deps=[t_mm, t_tt[3]], inc=True)
        t_oplast = t_oplastprev = t_opev[(sc, 3, 3)]
        t_x2store[sc] = DMA(SY, d_x2[512 * sc:512 * (sc + 1), :].rearrange(
            "(t p) d -> p t d", p=128),
            fv(CX1, 8192).rearrange("p (t d) -> p t d", d=D), d_x2w,
            deps=[t_oplast])

    t_x2F = Tok(SY, d_x2w, d_x2w.n)
    st["xn1_free"] = t_x2F
    st["w1j"][30] = st["w1j"][31] = st2["wo"][7]
    st["w2j"][30] = st["w2j"][31] = st2["wo"][7]
    c_stores = []
    cg_deps = {gg: [t_x2F] for gg in range(GROUPS)}
    cg_deps[0] += [t_oplast, t_hmlast]
    emit_ffn(d_x2, x_out, d_xc, d_out, cg_deps[0], c_stores,
             g_deps={1: cg_deps[1], 2: cg_deps[2], 3: cg_deps[3]})
    sch.wait(SY, d_out, d_out.n)

    _realize(nc, sch)
    es.close()
    return nc


def _realize(nc, sch):
    with nc.Block() as block:
        for eng_name, starter in [("sync", block.sync), ("tensor", block.tensor),
                                  ("vector", block.vector),
                                  ("scalar", block.scalar),
                                  ("gpsimd", block.gpsimd)]:
            def make(ops):
                def body(eng):
                    for f in ops:
                        f(eng)
                return body
            starter(make(sch.ops[eng_name]))


_BF = ml_dtypes.bfloat16


def _shard_inputs(L0_attn_x, L0_attn_norm_scale, L0_attn_conv_w, L0_attn_conv_b,
                  L0_attn_in_proj_w, L0_attn_dt_bias, L0_attn_head_mix,
                  L0_attn_out_proj_w, L0_ffn_w1, L0_ffn_b1, L0_ffn_w2, L0_ffn_b2):
    f32 = lambda a: np.ascontiguousarray(np.asarray(a), dtype=np.float32)
    X = f32(L0_attn_x)
    W1T = np.ascontiguousarray(f32(L0_ffn_w1).T)
    W2T = np.ascontiguousarray(f32(L0_ffn_w2).T)
    WPT = np.ascontiguousarray(f32(L0_attn_in_proj_w).T)
    WOT = np.ascontiguousarray(f32(L0_attn_out_proj_w).T)
    nsr = f32(L0_attn_norm_scale).reshape(1, D)
    cwc = np.ascontiguousarray(
        f32(L0_attn_conv_w).reshape(16, 128, 4).transpose(1, 0, 2).reshape(128, 64))
    cbc = np.ascontiguousarray(f32(L0_attn_conv_b).reshape(16, 128).T)
    dtbc = np.ascontiguousarray(f32(L0_attn_dt_bias).reshape(16, 128).T)
    b1c = np.ascontiguousarray(f32(L0_ffn_b1).reshape(64, 128).T)
    b2r = f32(L0_ffn_b2).reshape(1, D)
    hmr = np.ascontiguousarray(f32(L0_attn_head_mix).reshape(1, 256))
    eye = np.eye(128, dtype=_BF)
    ones1 = np.ones((1, 128), dtype=_BF)
    tri32 = np.triu(np.ones((32, 32), np.float32)).astype(_BF)
    stri32 = np.triu(np.ones((32, 32), np.float32), 1).astype(_BF)
    shared = dict(wpT=WPT, woT=WOT, nsr=nsr, cwc=cwc, cbc=cbc, dtbc=dtbc,
                  b1c=b1c, b2r=b2r, hmr=hmr, eye=eye, ones1=ones1,
                  tri32=tri32, stri32=stri32)
    maps = []
    for c in range(8):
        b, gr = c // 4, c % 4
        m = dict(shared)
        m["x"] = np.ascontiguousarray(X[b, gr * NT:(gr + 1) * NT, :])
        m["w1t_sl"] = np.ascontiguousarray(W1T[256 * c:256 * (c + 1), :])
        m["w2t_sl"] = np.ascontiguousarray(W2T[1024 * c:1024 * (c + 1), :])
        pos = (np.arange(NT, dtype=np.float32) + gr * NT)
        m["posc"] = np.ascontiguousarray(pos.reshape(16, 128).T)
        hsel = np.zeros((24, 3), np.float32)
        if gr > 0:
            for i in range(3):
                hsel[3 * (c - 1) + i, i] = 1.0
        m["hsel"] = hsel
        csel = np.zeros((32, 8), np.float32)
        for j in range(8):
            csel[8 * gr + j, j] = 1.0
        m["csel"] = csel
        maps.append(m)
    return maps


_NC_CACHE = {}


def kernel(**inputs):
    maps = _shard_inputs(**inputs)
    key = _STOP
    if key not in _NC_CACHE:
        _NC_CACHE[key] = build(stop=_STOP)
    nc = _NC_CACHE[key]
    res = run_bass_kernel_spmd(nc, maps, core_ids=list(range(8)))
    shards = [np.asarray(r["out"]) for r in res.results]
    out = np.empty((2, 8192, D), np.float32)
    for c in range(8):
        out[c // 4, (c % 4) * NT:(c % 4 + 1) * NT, :] = shards[c]
    return out


def _PART3(c):
    raise NotImplementedError
